# version 05
# baseline (speedup 1.0000x reference)
import numpy as np
import ml_dtypes

import concourse.bass as bass
import concourse.mybir as mybir
from concourse.bass_utils import run_bass_kernel_spmd

F32 = mybir.dt.float32
BF16 = mybir.dt.bfloat16
FP8 = mybir.dt.float8e4
ALU = mybir.AluOpType
ACT = mybir.ActivationFunctionType
AX = mybir.AxisListType
BF = ml_dtypes.bfloat16

ENGS = ("sync", "gpsimd", "vector", "scalar", "tensor")
SQ_W, SQ_S, SQ_G = "sqw", "sqs", "sqg"
SE_T, SE_V, SE_S, SE_G, SE_C = "st", "sv", "ss", "sg", "scc"
ALL_SEMS = (SQ_W, SQ_S, SQ_G, SE_T, SE_V, SE_S, SE_G, SE_C)


class Cfg:
    def __init__(self, D=4096, NH=64, T=1024, NC=8):
        self.D, self.NH, self.T, self.NC = D, NH, T, NC
        self.HD, self.S, self.Q, self.HALO = 64, 128, 128, 3
        self.DS = NH * self.HD
        assert self.DS == D
        self.CONV = self.DS + 2 * self.S
        self.PROJ = 2 * self.DS + 2 * self.S + NH
        self.KT = D // 128
        self.ZT = self.DS // 128
        self.XT = self.CONV // 128
        self.OT = self.ZT + self.XT + 1
        self.NCH = T // self.Q
        self.TH = T + self.HALO
        self.SLC = 6 * D // NC // 6
        self.MODW = 6 * self.SLC
        assert self.SLC % 128 == 0


class Prog:
    def __init__(self):
        self.ops = {e: [] for e in ENGS}
        self.cnt = {s: 0 for s in ALL_SEMS}
        self.sems = {}
        self.waited = {e: {} for e in ENGS}

    def inc(self, name, by=1):
        self.cnt[name] += by
        return self.cnt[name]

    def val(self, name):
        return self.cnt[name]

    def emit(self, eng, fn, waits=(), inc=None, inc_by=1):
        w = tuple((s, v) for (s, v) in waits if v > 0)
        post = None
        if inc is not None:
            self.inc(inc, inc_by)
            post = (inc, inc_by)

        def run(engine, _fn=fn, _w=w, _post=post, _eng=eng):
            wd = self.waited[_eng]
            for sname, v in _w:
                if wd.get(sname, 0) < v:
                    engine.wait_ge(self.sems[sname], v)
                    wd[sname] = v
            ins = _fn(engine)
            if _post is not None:
                ins.then_inc(self.sems[_post[0]], _post[1])
            return ins

        self.ops[eng].append(run)

    def wait(self, eng, waits):
        self.emit(eng, lambda e: None, waits=waits)

    def drain(self, eng):
        self.ops[eng].append(lambda e: e.drain())


def build(cfg: Cfg):
    nc = bass.Bass("TRN2", target_bir_lowering=False)
    _ncd = nc.allow_non_contiguous_dma(reason="small scatter/bcast DMAs")
    _ncd.__enter__()
    P = Prog()
    D, NH, HD, S, Q, T = cfg.D, cfg.NH, cfg.HD, cfg.S, cfg.Q, cfg.T
    KT, ZT, XT, OT, NCH, TH = cfg.KT, cfg.ZT, cfg.XT, cfg.OT, cfg.NCH, cfg.TH
    DS, CONV, PROJ, NC, SLC = cfg.DS, cfg.CONV, cfg.PROJ, cfg.NC, cfg.SLC
    HALO = cfg.HALO
    MODW = cfg.MODW

    x_in = nc.dram_tensor("x", [TH, D], F32, kind="ExternalInput")
    ipw = nc.dram_tensor("inproj", [D, PROJ], F32, kind="ExternalInput")
    opw = nc.dram_tensor("outproj", [DS, D], F32, kind="ExternalInput")
    ffw = nc.dram_tensor("ffnw", [D, D], F32, kind="ExternalInput")
    wmod = nc.dram_tensor("wmodslc", [D, MODW], F32, kind="ExternalInput")
    convw_in = nc.dram_tensor("convw", [CONV, 4], F32, kind="ExternalInput")
    VO, off = {}, 0
    for nm, ln in (("lnw", D), ("normw", DS), ("mup", OT * 128), ("ffnb", D),
                   ("alog", NH), ("dtb", NH), ("dvec", NH), ("cvec", D),
                   ("halo", HALO), ("segm", NC)):
        VO[nm] = (off, ln)
        off += ln
    vecs = nc.dram_tensor("vecs", [1, off], F32, kind="ExternalInput")
    out = nc.dram_tensor("out", [T, D], F32, kind="ExternalOutput")

    z_scr = nc.dram_tensor("z_scr", [ZT, 128, T], BF16)
    x2_scr = nc.dram_tensor("x2_scr", [T, D], F32)
    h2_scr = nc.dram_tensor("h2_scr", [T, D], F32)
    h2bf = nc.dram_tensor("h2bf", [T, D], BF16)
    wqbf = nc.dram_tensor("wqbf", [D, D], BF16)
    xnat_scr = nc.dram_tensor("xnat_scr", [T // 128, 128, DS], BF16)
    RW = max(NH * Q, 2 * D, 256)
    rows_scr = nc.dram_tensor("rows_scr", [16, RW], F32)
    ag1_in = nc.dram_tensor("ag1_in", [1, MODW], F32)
    ag1_out = nc.dram_tensor("ag1_out", [NC, MODW], F32)
    ag2_in = nc.dram_tensor("ag2_in", [S + 1, DS], F32)
    ag2_out = nc.dram_tensor("ag2_out", [NC * (S + 1), DS], F32)
    ag3_in = nc.dram_tensor("ag3_in", [1, 16], F32)
    if getattr(cfg, "dbg", False):
        dbg_y = nc.dram_tensor("dbg_y", [128, NCH * DS], BF16)
        dbg_r3 = nc.dram_tensor("dbg_r3", [128, NCH * DS], BF16)
        dbg_hin = nc.dram_tensor("dbg_hin", [128, DS], F32)
    ag3_out = nc.dram_tensor("ag3_out", [NC, 16], F32)

    ident_bf = nc.inline_tensor(np.eye(128, dtype=BF).view(np.uint16), name="idb")
    ident_f = nc.inline_tensor(np.eye(NH, dtype=np.float32), name="idf_")
    ones_c = nc.inline_tensor(np.ones((128, 1), dtype=BF).view(np.uint16), name="on1")
    trin = np.where(np.tril(np.ones((128, 128))).T >= 1, 0.0, -1e30)
    tri_c = nc.inline_tensor(trin.astype(np.float32), name="tri_")

    A = nc.alloc_sbuf_tensor
    poolC = A("poolC", [128, max(KT * (TH + 1), NCH * DS, KT * T,
                                 2 * MODW + 4 * SLC)], BF16)
    poolF = A("poolF", [128, 2 * max(D, 2048)], F32)
    DW = max(D, 2048)
    H0, H1 = poolF[:, 0:D], poolF[:, DW:DW + D]
    LBW = min(8 * Q, 2048)
    LBC = A("lbc", [128, LBW], F32)
    xstage = [poolF[:, 0:D], poolF[:, DW:DW + D]]
    fbig = [poolF[:, 0:D], poolF[:, DW:DW + D]]
    h2tile = poolF[:, 0:D]
    CBW = min(D, 1024)
    cb1B = poolF[:, DW:DW + CBW]
    sbB = poolF[:, DW + CBW:DW + 2 * CBW]

    ring = A("ring", [128, 3, TH], BF16)
    wst = A("wst", [128, 2, KT, 128], BF16)
    KH = KT // 2
    tstg = A("tstg", [128, 2, D], BF16)
    zst = A("zst", [128, 2, T], BF16)
    sqt = A("sqt", [128, 2, 128], BF16)
    stg = A("stg", [128, 2, 1024], F32)
    xtil = A("xtil", [128, DS], BF16)
    hbf = A("hbf", [128, DS], BF16)
    xns = tstg
    xblk = A("xblk", [128, 2, max(T, 1024)], BF16)
    h2q8 = A("h2q8", [128, 1, D], FP8)
    wq8 = h2q8
    wmh = poolC[:, 0:2 * MODW].rearrange("p (b w) -> p b w", b=2)
    wml = poolC[:, 2 * MODW:2 * MODW + 4 * SLC].rearrange("p (b w) -> p b w", b=2)
    wmf = stg

    lnw_c = A("lnw_c", [128, KT], F32)
    nw_c = A("nw_c", [128, KT], F32)
    mup_c = A("mup_c", [128, OT], F32)
    ffnb_c = A("ffnb_c", [128, KT], F32)
    cvec_c = A("cvec_c", [128, KT], F32)
    sc_hi = A("sc_hi", [128, KT], BF16)
    sc_lo = A("sc_lo", [128, KT], BF16)
    convw_c = A("convw_c", [128, XT, 4], F32)
    sa_c = A("sa_c", [128, KT], F32)
    ca1_c = A("ca1_c", [128, KT], F32)
    ga_c = A("ga_c", [128, KT], F32)
    sb_c2 = A("sb_c2", [128, KT], F32)
    cb1_c = A("cb1_c", [128, KT], F32)
    gb_c = A("gb_c", [128, KT], F32)
    dhead = A("dhead", [128, NH], F32)
    halo_c = A("halo_c", [128, HALO], F32)
    segm_t = A("segm_t", [NH, NC], F32)
    idbf = A("idbf", [128, 128], BF16)
    idf = A("idf", [NH, NH], F32)
    onec = A("onec", [128, 1], BF16)
    tri = A("tri", [128, 128], F32)
    drow = A("drow", [NH, 2, T + 2], F32)
    bt = A("bt", [128, T], BF16)
    ct = A("ct", [128, T], BF16)
    bnat = A("bnat", [128, 128], BF16)
    scr64 = A("scr64", [NH, 3 * Q], F32)
    ltc = A("ltc", [128, NH], F32)
    lndtc = A("lndtc", [128, NH], F32)
    expuc = A("expuc", [128, NH], F32)
    alphc = A("alphc", [128, NH], F32)
    gamt = A("gamt", [128, NH], F32)
    etmp = A("etmp", [128, 2, 128], F32)
    ebf = A("ebf", [128, 2, 128], BF16)
    wbf = A("wbf", [128, 4, 128], BF16)
    gbf = A("gbf", [128, 128], BF16)
    arow = A("arow", [NH, 8], F32)
    arow2 = A("arow2", [NH, NC], F32)
    wcomb = A("wcomb", [NH, NC], F32)
    wct = A("wct", [NC, NH], F32)
    hin_t = A("hin_t", [128, NH], F32)
    statc = A("statc", [128, 64], F32)
    sumc = A("sumc", [128, 2, NCH], F32)
    amx = A("amx", [128, 8], F32)
    RWT = max(TH + 1, 1400)
    rowt = A("rowt", [1, RWT], F32)
    gaB = A("gaB", [128, 256], F32)
    gbB = A("gbB", [128, 256], F32)
    bofs = A("bofs", [128, 256], F32)

    womb = [bass.AP(wst, 0, [[2 * KT * 128, 128], [256, KT], [1, 256]]),
            bass.AP(tstg, 0, [[2 * D, 128], [256, KT], [1, 256]])]

    PA = nc.alloc_psum_tensor("PA", [128, 512], F32)
    PB = nc.alloc_psum_tensor("PB", [128, 512], F32)
    PYA = nc.alloc_psum_tensor("PYA", [128, 1024], F32)
    PYB = nc.alloc_psum_tensor("PYB", [128, 1024], F32)
    PDH = nc.alloc_psum_tensor("PDH", [128, 512], F32)
    PTB = nc.alloc_psum_tensor("PTB", [128, 1024], BF16)
    ps_pair = [PA, PB]

    def rhs2_kt(k):
        return poolC[:, k * (TH + 1):k * (TH + 1) + TH]

    def ynat(c):
        return poolC[:, c * DS:(c + 1) * DS]

    def rhs3_slab(ft, st):
        return poolC[:, st * DS + ft * 128: st * DS + ft * 128 + 128]

    def h2qT(k):
        return poolC[:, k * T:(k + 1) * T]

    def dma(q, outap, inap, waits=(), **kw):
        sem = {"sync": SQ_W, "scalar": SQ_S, "gpsimd": SQ_G}[q]
        waits = list(waits) + [(sem, P.val(sem))]
        P.emit(q, lambda e, o=outap, i=inap, kw=kw: e.dma_start(out=o, in_=i, **kw),
               waits=waits, inc=sem, inc_by=16)
        return P.val(sem)

    def vop(fn, waits=(), dr=False):
        P.emit("vector", fn, waits=waits, inc=SE_V)
        if dr:
            P.drain("vector")
        return P.val(SE_V)

    def sop(fn, waits=(), dr=False):
        P.emit("scalar", fn, waits=waits, inc=SE_S)
        if dr:
            P.drain("scalar")
        return P.val(SE_S)

    def top(fn, waits=()):
        P.emit("tensor", fn, waits=waits, inc=SE_T)
        return P.val(SE_T)

    def bcast_tile(dst, dram_row_ap, waits=()):
        return dma("gpsimd", dst, dram_row_ap, waits=waits)

    def vsl(nm, a=0, b=None):
        o, ln = VO[nm]
        if b is None:
            b = ln
        return vecs[0:1, o + a:o + b]

    epsc = A("epsc", [128, 1], F32)
    P.emit("gpsimd", lambda e: e.memset(epsc[:, :], 1e-5), inc=SE_G)
    sg_eps = P.val(SE_G)

    dma("sync", idbf[:].bitcast(mybir.dt.uint16), ident_bf.ap())
    dma("sync", idf[:], ident_f.ap())
    dma("sync", onec[:].bitcast(mybir.dt.uint16), ones_c.ap())
    dma("sync", tri[:], tri_c.ap())

    def scatter_cols(dst, nm, ntile):
        o, ln = VO[nm]
        dma("gpsimd", dst[:, 0:ntile],
            bass.AP(vecs, o, [[1, 128], [128, ntile]]))

    scatter_cols(lnw_c, "lnw", KT)
    scatter_cols(nw_c, "normw", KT)
    scatter_cols(mup_c, "mup", OT)
    scatter_cols(ffnb_c, "ffnb", KT)
    scatter_cols(cvec_c, "cvec", KT)
    v2 = P.val(SQ_G)
    for k in range(4):
        dma("gpsimd", convw_c[:, :, k],
            bass.AP(convw_in, k, [[4, 128], [512, XT]]))
    dma("gpsimd", arow[0:NH, 0:1], bass.AP(vecs, VO["alog"][0], [[1, NH]]))
    g0 = dma("gpsimd", arow[0:NH, 2:3], bass.AP(vecs, VO["dtb"][0], [[1, NH]]))
    ss = sop(lambda e: e.activation(arow[0:NH, 1:2], arow[0:NH, 0:1], ACT.Exp),
             waits=[(SQ_G, g0)])
    sv = vop(lambda e: e.tensor_scalar_mul(arow[0:NH, 0:1], arow[0:NH, 1:2], -1.0),
             waits=[(SE_S, ss)])
    dma("gpsimd", dhead[:, 0:NH], vsl("dvec").broadcast_to([128, NH]))
    dma("gpsimd", halo_c[:, :], vsl("halo").broadcast_to([128, HALO]))
    g_const = dma("gpsimd", segm_t[0:NH, :], vsl("segm").broadcast_to([NH, NC]))
    s_arow = (sv, ss)

    if getattr(cfg, "stop_after", 99) < 0.5:
        return _finish(nc, P)

    ss = sop(lambda e: e.activation(cvec_c[:, 0:KT], cvec_c[:, 0:KT], ACT.Silu),
             waits=[(SQ_G, v2)])
    vop(lambda e: e.tensor_copy(sc_hi[:, 0:KT], cvec_c[:, 0:KT]),
        waits=[(SE_S, ss)], dr=True)
    sv_sc = vop(lambda e: e.tensor_sub(sc_lo[:, 0:KT], cvec_c[:, 0:KT],
                                       sc_hi[:, 0:KT]), dr=True)
    modps = [PA[0:1, 0:SLC], PB[0:1, 0:SLC], PYA[0:1, 0:SLC],
             PYA[0:1, 512:512 + SLC], PYB[0:1, 0:SLC], PYB[0:1, 512:512 + SLC]]
    GA_V, GB_V = 2, 5
    t_after_k = []
    for k in range(KT):
        b = k % 2
        wts = [(SE_T, t_after_k[k - 2])] if k >= 2 else []
        vh = dma("gpsimd", wmh[:, b, :], wmod[k * 128:(k + 1) * 128, :], waits=wts)
        vf1 = dma("sync", wmf[:, b, 0:SLC],
                  wmod[k * 128:(k + 1) * 128, GA_V * SLC:(GA_V + 1) * SLC], waits=wts)
        vf2 = dma("sync", wmf[:, b, SLC:2 * SLC],
                  wmod[k * 128:(k + 1) * 128, GB_V * SLC:(GB_V + 1) * SLC])
        vop(lambda e, b=b: e.tensor_sub(wml[:, b, 0:SLC], wmf[:, b, 0:SLC],
                                        wmh[:, b, GA_V * SLC:(GA_V + 1) * SLC]),
            waits=[(SQ_G, vh), (SQ_W, vf2)], dr=True)
        sv = vop(lambda e, b=b: e.tensor_sub(wml[:, b, SLC:2 * SLC],
                                             wmf[:, b, SLC:2 * SLC],
                                             wmh[:, b, GB_V * SLC:(GB_V + 1) * SLC]),
                 dr=True)
        for v6 in range(6):
            top(lambda e, k=k, b=b, v6=v6: e.matmul(
                modps[v6], sc_hi[:, k:k + 1], wmh[:, b, v6 * SLC:(v6 + 1) * SLC],
                start=(k == 0), stop=(k == KT - 1 and v6 not in (GA_V, GB_V))),
                waits=([(SE_V, sv)] if v6 == 0 else []))
        for v6, lo in ((GA_V, 0), (GB_V, SLC)):
            top(lambda e, k=k, b=b, v6=v6: e.matmul(
                modps[v6], sc_lo[:, k:k + 1], wmh[:, b, v6 * SLC:(v6 + 1) * SLC],
                start=False, stop=False))
            top(lambda e, k=k, b=b, v6=v6, lo=lo: e.matmul(
                modps[v6], sc_hi[:, k:k + 1], wml[:, b, lo:lo + SLC],
                start=False, stop=(k == KT - 1)))
        t_after_k.append(P.val(SE_T))
    st_mod = P.val(SE_T)
    mod_dma = {}
    for v6 in range(6):
        slot = (v6 % 2) * 512
        wts = [(SE_T, st_mod)] if v6 == 0 else []
        if v6 >= 2:
            wts.append((SQ_G, mod_dma[v6 - 2]))
        sv = vop(lambda e, v6=v6, slot=slot: e.tensor_copy(
            rowt[0:1, slot:slot + SLC], modps[v6]), waits=wts)
        mod_dma[v6] = dma("gpsimd", ag1_in[0:1, v6 * SLC:(v6 + 1) * SLC],
                          rowt[0:1, slot:slot + SLC], waits=[(SE_V, sv)])
    v_ag1 = mod_dma[5]

    if getattr(cfg, "stop_after", 99) < 1.0:
        return _finish(nc, P)
    P.emit("gpsimd",
           lambda e: e.collective_compute(
               "AllGather", ALU.bypass, replica_groups=[list(range(NC))],
               ins=[ag1_in[:, :].opt()], outs=[ag1_out[:, :].opt()]),
           waits=[(SQ_G, v_ag1)], inc=SE_C)
    agc1 = P.val(SE_C)
    spt = SLC // 128
    for v6, dst in ((0, sa_c), (1, ca1_c), (2, ga_c), (3, sb_c2), (4, cb1_c),
                    (5, gb_c)):
        for r in range(NC):
            dma("gpsimd", dst[:, r * spt:(r + 1) * spt],
                bass.AP(ag1_out, (r * 6 + v6) * SLC, [[1, 128], [128, spt]]),
                waits=[(SE_C, agc1)])
    g_mod = P.val(SQ_G)

    if getattr(cfg, "stop_after", 99) < 1.5:
        return _finish(nc, P)
    vop(lambda e: e.tensor_scalar_add(ca1_c[:, 0:KT], ca1_c[:, 0:KT], 1.0),
        waits=[(SQ_G, g_mod)], dr=True)
    vop(lambda e: e.tensor_scalar_add(cb1_c[:, 0:KT], cb1_c[:, 0:KT], 1.0), dr=True)
    for g in (ga_c, gb_c):
        sv = vop(lambda e, g=g: e.tensor_scalar(statc[:, 0:KT], g[:, 0:KT], 0.0,
                                                None, ALU.is_ge), dr=True)
        vop(lambda e: e.tensor_scalar(statc[:, 32:32 + KT], statc[:, 0:KT],
                                      2.0, 1.0, ALU.mult, ALU.subtract), dr=True)
        ssx = sop(lambda e, g=g: e.activation(g[:, 0:KT], g[:, 0:KT], ACT.Abs),
                  waits=[(SE_V, sv)])
        vop(lambda e, g=g: e.tensor_scalar_max(g[:, 0:KT], g[:, 0:KT], 0.25),
            waits=[(SE_S, ssx)], dr=True)
        vop(lambda e, g=g: e.tensor_mul(g[:, 0:KT], g[:, 0:KT],
                                        statc[:, 32:32 + KT]), dr=True)
    s_modcols = P.val(SE_V)
    for ri, g in ((4, ga_c), (5, gb_c)):
        dma("gpsimd", bass.AP(rows_scr, ri * RW, [[0, 1], [1, 128], [128, KT]]),
            g[:, 0:KT], waits=[(SE_V, s_modcols)])
    g_rows45 = P.val(SQ_G)

    if getattr(cfg, "stop_after", 99) < 2:
        return _finish(nc, P)
    groups = [(0, HALO)] + [(HALO + i * 128, 128) for i in range(NCH)]
    s_use = {0: 0, 1: 0}
    NDCH = (D + T - 1) // T
    for gi, (r0, nr) in enumerate(groups):
        b = gi % 2
        wl = [(SE_S, s_use[b])] if gi >= 2 else []
        vx = dma("sync", xstage[b][0:nr, :], x_in[r0:r0 + nr, :], waits=wl)
        w0 = [(SQ_W, vx), (SE_V, s_modcols)]
        vop(lambda e, b=b, nr=nr: e.reduce_sum(statc[0:nr, 0:1],
                                               xstage[b][0:nr, :], axis=AX.X),
            waits=w0, dr=True)
        for j in range(NDCH):
            c0, c1 = j * T, min(D, (j + 1) * T)
            vop(lambda e, b=b, nr=nr, c0=c0, c1=c1, j=j: e.scalar_tensor_tensor(
                stg[:, 0, 0:c1 - c0][0:nr, :], xstage[b][0:nr, c0:c1], 1.0,
                xstage[b][0:nr, c0:c1], ALU.mult, ALU.mult,
                accum_out=statc[0:nr, 1 + j:2 + j]),
                dr=True)
        vop(lambda e, nr=nr: e.reduce_sum(statc[0:nr, 8:9],
                                          statc[0:nr, 1:1 + NDCH], axis=AX.X),
            dr=True)
        vop(lambda e, nr=nr: e.tensor_scalar_mul(statc[0:nr, 9:10],
                                                 statc[0:nr, 0:1], 1.0 / D), dr=True)
        vop(lambda e, nr=nr: e.scalar_tensor_tensor(
            statc[0:nr, 10:11], statc[0:nr, 9:10], -1.0, statc[0:nr, 9:10],
            ALU.mult, ALU.mult), dr=True)
        sv = vop(lambda e, nr=nr: e.scalar_tensor_tensor(
            statc[0:nr, 11:12], statc[0:nr, 8:9], 1.0 / D, statc[0:nr, 10:11],
            ALU.mult, ALU.add), dr=True)
        sop(lambda e, nr=nr: e.activation(statc[0:nr, 12:13], statc[0:nr, 11:12],
                                          ACT.Ln, bias=epsc[0:nr, 0:1]),
            waits=[(SE_V, sv), (SE_G, sg_eps)], dr=True)
        ss = sop(lambda e, nr=nr: e.activation(statc[0:nr, 13:14],
                                               statc[0:nr, 12:13],
                                               ACT.Exp, scale=-0.5), dr=True)
        sv = vop(lambda e, nr=nr: e.scalar_tensor_tensor(
            statc[0:nr, 14:15], statc[0:nr, 9:10], -1.0, statc[0:nr, 13:14],
            ALU.mult, ALU.mult), waits=[(SE_S, ss)], dr=True)
        if gi == 1 and getattr(cfg, "sub", 99) < 1:
            return _finish(nc, P)
        ss = sop(lambda e, b=b, nr=nr: e.activation(
            tstg[:, b, :][0:nr, :], xstage[b][0:nr, :], ACT.Identity,
            bias=statc[0:nr, 14:15], scale=statc[0:nr, 13:14]),
            waits=[(SE_V, sv)])
        s_use[b] = ss
        if gi == 1 and getattr(cfg, "sub", 99) < 2:
            return _finish(nc, P)
        sv_prev_evac = 0
        for kt in range(KT):
            if gi == 1 and kt > getattr(cfg, "sub", 99):
                break
            slot = kt % 8
            wts = [(SE_S, ss)] if kt == 0 else []
            if kt >= 1:
                wts = wts + [(SE_S, P.val(SE_S))]
            stv = top(lambda e, b=b, kt=kt, nr=nr, slot=slot: e.matmul(
                PTB[:, slot * 128:slot * 128 + nr],
                tstg[:, b, :][0:nr, kt * 128:(kt + 1) * 128],
                idbf[0:nr, 0:nr], is_transpose=True), waits=wts)
            sop(lambda e, kt=kt, r0=r0, nr=nr, slot=slot: e.activation(
                rhs2_kt(kt)[:, r0:r0 + nr], PTB[:, slot * 128:slot * 128 + nr],
                ACT.Identity, bias=sa_c[:, kt:kt + 1], scale=ca1_c[:, kt:kt + 1]),
                waits=[(SE_T, stv)])
        s_use[b] = P.val(SE_S)
        if getattr(cfg, "stop_after", 99) < 2.05 + gi * 0.01:
            return _finish(nc, P)
    s_h1 = P.val(SE_S)
    if getattr(cfg, "stop_after", 99) < 2.2:
        return _finish(nc, P)
    CW2 = min(512, T)
    ncc = (TH + CW2 - 1) // CW2
    for cc in range(ncc):
        c0, c1 = cc * CW2, min(TH, cc * CW2 + CW2)
        for kt in range(KT):
            sq = kt % 2
            wts = [(SE_S, s_h1)] if (cc == 0 and kt == 0) else []
            if kt >= 2:
                wts = wts + [(SE_T, P.val(SE_T))]
            ss = sop(lambda e, kt=kt, c0=c0, c1=c1, sq=sq: e.activation(
                zst[:, sq, 0:c1 - c0], rhs2_kt(kt)[:, c0:c1], ACT.Square),
                waits=wts)
            top(lambda e, c0=c0, c1=c1, sq=sq, kt=kt: e.matmul(
                PA[0:1, 0:c1 - c0], onec[:, 0:1], zst[:, sq, 0:c1 - c0],
                start=(kt == 0), stop=(kt == KT - 1)),
                waits=[(SE_S, ss)])
        stv = P.val(SE_T)
        sop(lambda e, c0=c0, c1=c1: e.activation(
            rowt[0:1, c0:c1], PA[0:1, 0:c1 - c0], ACT.Ln,
            bias=epsc[0:1, 0:1], scale=1.0 / DS),
            waits=[(SE_T, stv), (SE_G, sg_eps), (SQ_G, v_ag1)], dr=True)
        sop(lambda e, c0=c0, c1=c1: e.activation(
            rowt[0:1, c0:c1], rowt[0:1, c0:c1], ACT.Exp, scale=-0.5))
    ss = P.val(SE_S)
    if getattr(cfg, "stop_after", 99) < 2.4:
        return _finish(nc, P)
    v1 = dma("scalar", rows_scr[0:1, 0:TH], rowt[0:1, 0:TH], waits=[(SE_S, ss)])
    for cc0 in range(0, TH, LBW):
        cl = min(LBW, TH - cc0)
        v2 = dma("gpsimd", LBC[:, 0:cl],
                 bass.AP(rows_scr, cc0, [[0, 128], [1, cl]]),
                 waits=[(SQ_S, v1), (SE_V, P.val(SE_V))])
        for kt in range(KT):
            vop(lambda e, kt=kt, cc0=cc0, cl=cl: e.scalar_tensor_tensor(
                rhs2_kt(kt)[:, cc0:cc0 + cl], LBC[:, 0:cl], lnw_c[:, kt:kt + 1],
                rhs2_kt(kt)[:, cc0:cc0 + cl], ALU.mult, ALU.mult),
                waits=[(SQ_G, v2), (SE_S, ss)] if kt == 0 else [], dr=True)
    for kt in range(KT):
        vop(lambda e, kt=kt: e.tensor_mul(
            rhs2_kt(kt)[:, 0:HALO], rhs2_kt(kt)[:, 0:HALO], halo_c[:, 0:HALO]),
            dr=True)
    s_rhs2 = P.val(SE_V)

    if getattr(cfg, "stop_after", 99) < 3:
        return _finish(nc, P)
    if T % 512 == 0:
        CH = [(0, HALO)] + [(HALO + i * 512, 512) for i in range(T // 512)]
    else:
        CH = [(0, HALO)] + [(HALO + i * 256, 256) for i in range(T // 256)]
    order = list(range(ZT, ZT + XT)) + [OT - 1] + list(range(ZT))
    otw = {ot: (NH if ot == OT - 1 else 128) for ot in range(OT)}
    z_store_after = {}
    xblk_store = {}
    pscnt = [0]
    bank_evac = {0: 0, 1: 0}
    s_conv_after = {}
    s_xnat_after = {}
    t_gemm_after = []
    v_evac_after = []
    for oi, ot in enumerate(order):
        b = oi % 2
        wts = [(SE_S, v_evac_after[oi - 2][0]), (SE_V, v_evac_after[oi - 2][1])] \
            if oi >= 2 else []
        wv = dma("gpsimd", wst[:, b, :, 0:otw[ot]],
                 bass.AP(ipw, ot * 128,
                         [[PROJ, 128], [128 * PROJ, KT], [1, otw[ot]]]),
                 waits=[(SE_T, t_gemm_after[oi - 2])] if oi >= 2 else [])
        is_x = ZT <= ot < ZT + XT
        is_dt = ot == OT - 1
        t = ot - ZT
        chs = CH if is_x else CH[1:]
        for ci, (c0, cl) in enumerate(chs):
            pb = pscnt[0] % 2
            pscnt[0] += 1
            ps = ps_pair[pb]
            for kt in range(KT):
                wts = []
                if kt == 0:
                    wts = [(SQ_G, wv), (SE_V, s_rhs2),
                           (SE_S, bank_evac[pb])]
                top(lambda e, b=b, kt=kt, ot=ot, c0=c0, cl=cl, ps=ps: e.matmul(
                    ps[0:otw[ot], 0:cl], wst[:, b, kt, 0:otw[ot]],
                    rhs2_kt(kt)[:, c0:c0 + cl],
                    start=(kt == 0), stop=(kt == KT - 1)), waits=wts)
            stv = P.val(SE_T)
            if is_x:
                wts = [(SE_T, stv)]
                if t >= 3 and ci == 0:
                    wts.append((SE_V, s_conv_after.get(t - 3, 0)))
                sop(lambda e, t=t, ot=ot, c0=c0, cl=cl, ps=ps: e.activation(
                    ring[:, t % 3, c0:c0 + cl], ps[:, 0:cl], ACT.Identity,
                    scale=mup_c[:, ot:ot + 1]), waits=wts)
                bank_evac[pb] = P.val(SE_S)
            elif is_dt:
                sop(lambda e, c0=c0, cl=cl, ps=ps: e.activation(
                    stg[0:NH, 1, 0:cl], ps[0:NH, 0:cl], ACT.Exp,
                    scale=mup_c[0:NH, OT - 1:OT], bias=arow[0:NH, 2:3]),
                    waits=[(SE_T, stv), (SE_S, s_arow[1]), (SE_V, s_arow[0])],
                    dr=True)
                sop(lambda e, c0=c0, cl=cl: e.activation(
                    drow[0:NH, 0, c0 - HALO:c0 - HALO + cl],
                    stg[0:NH, 1, 0:cl], ACT.Ln, bias=1.0))
                bank_evac[pb] = P.val(SE_S)
            else:
                wts = [(SE_T, stv), (SE_V, P.val(SE_V))]
                if ci == 0 and z_store_after.get(oi - 2) is not None:
                    wts.append((SQ_S, z_store_after[oi - 2]))
                sop(lambda e, ot=ot, c0=c0, cl=cl, ps=ps, b=b: e.activation(
                    zst[:, b, c0 - HALO:c0 - HALO + cl], ps[:, 0:cl], ACT.Identity,
                    scale=mup_c[:, ot:ot + 1]), waits=wts)
                bank_evac[pb] = P.val(SE_S)
        v_evac_after.append((P.val(SE_S), P.val(SE_V)))
        t_gemm_after.append(P.val(SE_T))
        if is_x:
            ss = P.val(SE_S)
            vop(lambda e, t=t: e.tensor_scalar_mul(
                stg[:, 0, 0:T], ring[:, t % 3, 0:T], convw_c[:, t, 0:1]),
                waits=[(SE_S, ss)], dr=True)
            for k in range(1, 4):
                vop(lambda e, t=t, k=k: e.scalar_tensor_tensor(
                    stg[:, 0, 0:T], ring[:, t % 3, k:k + T],
                    convw_c[:, t, k:k + 1], stg[:, 0, 0:T], ALU.mult, ALU.add),
                    dr=True)
            sv = P.val(SE_V)
            s_conv_after[t] = sv
            if t == XT - 2:
                sop(lambda e: e.activation(bt[:, 0:T], stg[:, 0, 0:T], ACT.Silu),
                    waits=[(SE_V, sv)])
            elif t == XT - 1:
                sop(lambda e: e.activation(ct[:, 0:T], stg[:, 0, 0:T], ACT.Silu),
                    waits=[(SE_V, sv)])
            else:
                wts = [(SE_V, sv)]
                if t >= 2:
                    wts.append((SE_V, s_xnat_after.get(t - 2, 0)))
                ss2 = sop(lambda e, b=b: e.activation(
                    zst[:, b, 0:T], stg[:, 0, 0:T], ACT.Silu), waits=wts)
                for c in range(NCH):
                    slot = c % 8
                    wts = [(SE_S, ss2)] if c == 0 else []
                    if c == 0:
                        wts.append((SQ_S, xblk_store.get(t - 2, 0)))
                    if c >= 1:
                        wts = wts + [(SE_V, P.val(SE_V))]
                    stv = top(lambda e, b=b, c=c, slot=slot: e.matmul(
                        PTB[:, slot * 128:(slot + 1) * 128],
                        zst[:, b, c * 128:(c + 1) * 128], idbf[:, :],
                        is_transpose=True), waits=wts)
                    vop(lambda e, c=c, t=t, slot=slot: e.tensor_copy(
                        xblk[:, t % 2, c * 128:(c + 1) * 128],
                        PTB[:, slot * 128:(slot + 1) * 128]),
                        waits=[(SE_T, stv)])
                s_xnat_after[t] = P.val(SE_V)
                xblk_store[t] = dma(
                    "scalar",
                    bass.AP(xnat_scr, t * 128,
                            [[DS, 128], [128 * DS, NCH], [1, 128]]),
                    xblk[:, t % 2, 0:NCH * 128].rearrange(
                        "p (c q) -> p c q", q=128),
                    waits=[(SE_V, s_xnat_after[t])])
        elif not is_dt:
            ss = P.val(SE_S)
            z_store_after[oi] = dma("scalar", z_scr[ot, :, :], zst[:, b, 0:T],
                                    waits=[(SE_S, ss)])
    s_p3 = (P.val(SE_V), P.val(SE_S))
    vz_done = P.val(SQ_S)

    P.emit("gpsimd", lambda e: e.memset(drow[0:NH, 1, 0:1], 0.0),
           waits=[(SE_S, s_p3[1])], inc=SE_G)
    P.emit("gpsimd", lambda e: e.memset(stg[0:NH, 0, 0:T], 1.0),
           waits=[(SE_V, s_p3[0]), (SE_S, s_p3[1])], inc=SE_G)
    sg = P.val(SE_G)
    vop(lambda e: e.tensor_tensor_scan(
        drow[0:NH, 1, 1:1 + T], stg[0:NH, 0, 0:T], drow[0:NH, 0, 0:T],
        0.0, ALU.mult, ALU.add), waits=[(SE_G, sg), (SE_S, s_p3[1])], dr=True)
    vop(lambda e: e.tensor_scalar_mul(drow[0:NH, 1, 1:1 + T],
                                      drow[0:NH, 1, 1:1 + T], arow[0:NH, 0:1]),
        dr=True)
    sv = vop(lambda e: e.reduce_sum(arow[0:NH, 4:5], drow[0:NH, 0, 0:T],
                                    axis=AX.X), dr=True)
    s_dt = (P.val(SE_V), P.val(SE_S))

    def chunk_smalls(c):
        lc = drow[0:NH, 1, 1 + c * Q:1 + (c + 1) * Q]
        lprev = drow[0:NH, 1, c * Q:c * Q + 1]
        lend = drow[0:NH, 1, (c + 1) * Q:(c + 1) * Q + 1]
        ss = sop(lambda e: e.activation(scr64[0:NH, 0:Q], lc, ACT.Exp,
                                        scale=-1.0, bias=lend),
                 waits=[(SE_V, s_dt[0]), (SE_S, s_dt[1]),
                        (SE_T, P.val(SE_T))])
        vop(lambda e, c=c: e.tensor_mul(scr64[0:NH, 0:Q], scr64[0:NH, 0:Q],
                                        drow[0:NH, 0, c * Q:(c + 1) * Q]),
            waits=[(SE_S, ss)], dr=True)
        sv = vop(lambda e: e.tensor_scalar_mul(arow[0:NH, 5:6], lprev, -1.0),
                 dr=True)
        sop(lambda e: e.activation(scr64[0:NH, Q:2 * Q], lc, ACT.Exp,
                                   bias=arow[0:NH, 5:6]), waits=[(SE_V, sv)],
            dr=True)
        sop(lambda e: e.activation(arow[0:NH, 5:6], lend, ACT.Exp,
                                   bias=arow[0:NH, 5:6]), dr=True)
        ss = sop(lambda e, c=c: e.activation(
            scr64[0:NH, 2 * Q:3 * Q], drow[0:NH, 0, c * Q:(c + 1) * Q], ACT.Ln))
        pairs = [(scr64[0:NH, 0:Q], alphc), (scr64[0:NH, Q:2 * Q], expuc),
                 (lc, ltc), (scr64[0:NH, 2 * Q:3 * Q], lndtc)]
        for pi, (src, dst) in enumerate(pairs):
            stv = top(lambda e, src=src, pi=pi: e.matmul(
                PDH[:, pi * NH:(pi + 1) * NH], src, idf[:, :],
                is_transpose=True),
                waits=[(SE_S, ss), (SE_V, P.val(SE_V))] if pi == 0 else
                [(SE_V, P.val(SE_V))])
            vop(lambda e, dst=dst, pi=pi: e.tensor_copy(
                dst[:, 0:NH], PDH[:, pi * NH:(pi + 1) * NH]),
                waits=[(SE_T, stv)])
        v1 = dma("scalar", rows_scr[2:3, 0:NH], arow[0:NH, 5:6],
                 waits=[(SE_S, ss)])
        gv = dma("gpsimd", gamt[:, 0:NH],
                 bass.AP(rows_scr, 2 * RW, [[0, 128], [1, NH]]),
                 waits=[(SQ_S, v1)])
        return P.val(SE_V), gv

    def hbc(tile):
        return tile[:, 0:NH].unsqueeze(2).broadcast_to([128, NH, HD])

    xns_load = {}

    def load_xns(c, extra=()):
        xns_load[c] = dma("sync", xns[:, c % 2, :],
                          xnat_scr[c].rearrange("c p q -> p (c q)") if False
                          else xnat_scr[c, :, :],
                          waits=list(extra))

    def build_xtil(c, extra=()):
        vop(lambda e, c=c: e.tensor_tensor(
            xtil[:, 0:DS].rearrange("p (a b) -> p a b", b=HD),
            xns[:, c % 2, :].rearrange("p (a b) -> p a b", b=HD),
            hbc(alphc), ALU.mult),
            waits=list(extra) + [(SE_V, P.val(SE_V)), (SQ_W, xns_load[c])],
            dr=True)

    def bnat_of(c, extra=()):
        stv = top(lambda e, c=c: e.matmul(PTB[:, 0:128], bt[:, c * Q:(c + 1) * Q],
                                          idbf[:, :], is_transpose=True),
                  waits=list(extra) + [(SE_S, s_dt[1]), (SE_V, P.val(SE_V))])
        vop(lambda e: e.tensor_copy(bnat[:, :], PTB[:, 0:128]),
            waits=[(SE_T, stv)])

    def dh_update(c, hcur, hnew, gv):
        ng = DS // 512
        for g in range(ng):
            stv = top(lambda e, g=g: e.matmul(
                PDH[:, 0:512], bnat[:, :], xtil[:, g * 512:(g + 1) * 512],
                start=True, stop=True), waits=[(SE_V, P.val(SE_V))])
            hpg = 512 // HD
            vop(lambda e, g=g, hcur=hcur: e.tensor_tensor(
                stg[:, 0, 0:512].rearrange("p (a b) -> p a b", b=HD),
                hcur[:, g * 512:(g + 1) * 512].rearrange("p (a b) -> p a b", b=HD),
                gamt[:, g * hpg:(g + 1) * hpg].unsqueeze(2).broadcast_to(
                    [128, hpg, HD]), ALU.mult),
                waits=[(SE_T, stv), (SQ_G, gv)], dr=True)
            vop(lambda e, g=g, hnew=hnew: e.tensor_add(
                hnew[:, g * 512:(g + 1) * 512], stg[:, 0, 0:512], PDH[:, 0:512]),
                dr=True)

    if getattr(cfg, "stop_after", 99) < 4:
        return _finish(nc, P)
    P.emit("gpsimd", lambda e: e.memset(H0[:, 0:DS], 0.0),
           waits=[(SE_S, s_use[0]), (SE_S, s_use[1])], inc=SE_G)
    sg = P.val(SE_G)
    P.wait("vector", [(SE_G, sg)])
    hb = [H0, H1]
    for c in range(NCH):
        load_xns(c, [(SQ_S, xblk_store[XT - 3])] if c == 0 else
                 [(SE_V, P.val(SE_V))])
        sv, gv = chunk_smalls(c)
        bnat_of(c)
        build_xtil(c, [(SE_V, s_p3[0])])
        dh_update(c, hb[c % 2], hb[(c + 1) % 2], gv)
    hfin = hb[NCH % 2]
    sv = P.val(SE_V)
    P.emit("gpsimd", lambda e: e.memset(stg[0:1, 1, 0:1024], 0.0),
           waits=[(SE_S, P.val(SE_S))], inc=SE_G)
    sgz2 = P.val(SE_G)
    for seg in range(0, DS, 1024):
        dma("scalar", ag2_in[S:S + 1, seg:seg + min(1024, DS - seg)],
            stg[0:1, 1, 0:min(1024, DS - seg)], waits=[(SE_G, sgz2)])
    v1 = dma("scalar", ag2_in[0:S, :], hfin[0:S, 0:DS], waits=[(SE_V, sv)])
    v2 = dma("scalar", ag2_in[S:S + 1, 0:NH], arow[0:NH, 4:5])
    P.emit("gpsimd",
           lambda e: e.collective_compute(
               "AllGather", ALU.bypass, replica_groups=[list(range(NC))],
               ins=[ag2_in[:, :].opt()], outs=[ag2_out[:, :].opt()]),
           waits=[(SQ_S, v2)], inc=SE_C)
    agc2 = P.val(SE_C)

    if getattr(cfg, "stop_after", 99) < 5:
        return _finish(nc, P)
    v1 = dma("gpsimd", arow2[0:NH, 0:NC],
             bass.AP(ag2_out, S * DS, [[1, NH], [(S + 1) * DS, NC]]),
             waits=[(SE_C, agc2)])
    P.emit("gpsimd", lambda e: e.memset(stg[0:NH, 0, 0:NC], 1.0),
           waits=[(SE_V, P.val(SE_V))], inc=SE_G)
    sg5 = P.val(SE_G)
    vop(lambda e: e.tensor_tensor_scan(
        wcomb[0:NH, 0:NC], stg[0:NH, 0, 0:NC], arow2[0:NH, 0:NC],
        0.0, ALU.mult, ALU.add), waits=[(SQ_G, v1), (SE_G, sg5)], dr=True)
    vop(lambda e: e.tensor_mul(arow2[0:NH, 0:NC], arow2[0:NH, 0:NC],
                               segm_t[0:NH, 0:NC]),
        waits=[(SQ_G, g_const)], dr=True)
    vop(lambda e: e.reduce_sum(arow[0:NH, 5:6], arow2[0:NH, 0:NC], axis=AX.X),
        dr=True)
    sv = vop(lambda e: e.tensor_scalar(wcomb[0:NH, 0:NC], wcomb[0:NH, 0:NC],
                                       arow[0:NH, 5:6], 0.0,
                                       ALU.subtract, ALU.min), dr=True)
    ss = sop(lambda e: e.activation(wcomb[0:NH, 0:NC], wcomb[0:NH, 0:NC], ACT.Exp,
                                    scale=arow[0:NH, 1:2]), waits=[(SE_V, sv)])
    sv = vop(lambda e: e.tensor_mul(wcomb[0:NH, 0:NC], wcomb[0:NH, 0:NC],
                                    segm_t[0:NH, 0:NC]), waits=[(SE_S, ss)], dr=True)
    stv = top(lambda e: e.matmul(PDH[0:NC, 0:NH], wcomb[0:NH, 0:NC], idf[:, :],
                                 is_transpose=True), waits=[(SE_V, sv)])
    sv = vop(lambda e: e.tensor_copy(wct[0:NC, 0:NH], PDH[0:NC, 0:NH]),
             waits=[(SE_T, stv)])
    v1 = dma("scalar", rows_scr[8:8 + NC, 0:NH], wct[0:NC, 0:NH],
             waits=[(SE_V, sv)])
    P.emit("gpsimd", lambda e: e.memset(H0[:, 0:DS], 0.0),
           waits=[(SQ_S, v1)], inc=SE_G)
    sg = P.val(SE_G)
    P.wait("vector", [(SE_G, sg)])
    for j in range(NC):
        vw = dma("gpsimd", hin_t[:, 0:NH],
                 bass.AP(rows_scr, (8 + j) * RW, [[0, 128], [1, NH]]),
                 waits=[(SQ_S, v1), (SE_V, P.val(SE_V))])
        vs = dma("sync", H1[0:S, 0:DS], ag2_out[j * (S + 1):j * (S + 1) + S, :],
                 waits=[(SE_V, P.val(SE_V))])
        vop(lambda e: e.tensor_tensor(
            H1[:, 0:DS].rearrange("p (a b) -> p a b", b=HD),
            H1[:, 0:DS].rearrange("p (a b) -> p a b", b=HD),
            hbc(hin_t), ALU.mult), waits=[(SQ_G, vw), (SQ_W, vs)], dr=True)
        vop(lambda e: e.tensor_add(H0[:, 0:DS], H0[:, 0:DS], H1[:, 0:DS]), dr=True)
    s_hin = P.val(SE_V)
    if getattr(cfg, "dbg", False):
        dma("scalar", dbg_hin[:, :], H0[:, 0:DS], waits=[(SE_V, s_hin)])

    if getattr(cfg, "stop_after", 99) < 6:
        return _finish(nc, P)
    QW = min(DS, 1024)
    NQ = DS // QW
    HGRP = min(LBW // Q, NH)
    hb2 = [H0, H1]
    for c in range(NCH):
        load_xns(c, [(SE_V, P.val(SE_V))])
        sv, gv = chunk_smalls(c)
        bnat_of(c)
        stv = top(lambda e, c=c: e.matmul(
            PDH[:, 0:128], bt[:, c * Q:(c + 1) * Q], ct[:, c * Q:(c + 1) * Q],
            start=True, stop=True),
            waits=[(SE_V, P.val(SE_V))])
        vop(lambda e: e.tensor_copy(gbf[:, :], PDH[:, 0:128]),
            waits=[(SE_T, stv)])
        build_xtil(c)
        hcur, hnew = hb2[c % 2], hb2[(c + 1) % 2]
        vop(lambda e, hcur=hcur: e.tensor_copy(hbf[:, 0:DS], hcur[:, 0:DS]),
            waits=[(SE_V, s_hin if c == 0 else P.val(SE_V))], dr=True)
        lb_v = 0
        for q in range(NQ):
            f0 = q * QW
            h0h = f0 // HD
            nheads = QW // HD
            for g in range(QW // 512):
                top(lambda e, c=c, f0=f0, g=g: e.matmul(
                    PYB[:, g * 512:(g + 1) * 512], ct[:, c * Q:(c + 1) * Q],
                    hbf[:, f0 + g * 512:f0 + (g + 1) * 512],
                    start=True, stop=True),
                    waits=[(SE_V, P.val(SE_V))] if g == 0 else [])
            for hh in range(nheads):
                h = h0h + hh
                hl = h % HGRP
                if h % HGRP == 0:
                    hg = h // HGRP
                    vv1 = dma("scalar", rows_scr[3:4, 0:HGRP * Q],
                              bass.AP(drow, (hg * HGRP) * 2 * (T + 2) + (T + 2)
                                      + 1 + c * Q,
                                      [[2 * (T + 2), HGRP], [1, Q]]),
                              waits=[(SE_V, P.val(SE_V))])
                    lb_v = dma("gpsimd", LBC[:, 0:HGRP * Q],
                               bass.AP(rows_scr, 3 * RW,
                                       [[0, 128], [1, HGRP * Q]]),
                               waits=[(SQ_S, vv1), (SE_V, P.val(SE_V))])
                es, ws = hh % 2, hh % 4
                wts = [(SQ_G, lb_v)]
                if hh >= 2:
                    wts.append((SE_S, P.val(SE_S)))
                if hh >= 4:
                    wts.append((SE_T, P.val(SE_T)))
                vop(lambda e, hl=hl, h=h, es=es: e.scalar_tensor_tensor(
                    etmp[:, es, :], LBC[:, hl * Q:(hl + 1) * Q],
                    ltc[:, h:h + 1], tri[:, :], ALU.subtract, ALU.add),
                    waits=wts)
                sv = P.val(SE_V)
                ss = sop(lambda e, h=h, es=es: e.activation(
                    ebf[:, es, :], etmp[:, es, :], ACT.Exp,
                    bias=lndtc[:, h:h + 1]), waits=[(SE_V, sv)])
                vop(lambda e, es=es, ws=ws: e.tensor_mul(
                    wbf[:, ws, :], ebf[:, es, :], gbf[:, :]),
                    waits=[(SE_S, ss)], dr=True)
                sv = vop(lambda e, h=h, ws=ws: e.scalar_tensor_tensor(
                    wbf[:, ws, :], idbf[:, :], dhead[:, h:h + 1], wbf[:, ws, :],
                    ALU.mult, ALU.add), dr=True)
                top(lambda e, c=c, h=h, hh=hh, ws=ws: e.matmul(
                    PYA[:, hh * HD:(hh + 1) * HD], wbf[:, ws, :],
                    xns[:, c % 2, h * HD:(h + 1) * HD],
                    start=True, stop=True),
                    waits=[(SE_V, sv), (SQ_W, xns_load[c])])
            stv = P.val(SE_T)
            vop(lambda e, q=q, h0h=h0h, nheads=nheads: e.tensor_tensor(
                stg[:, 0, 0:QW].rearrange("p (a b) -> p a b", b=HD),
                PYB[:, 0:QW].rearrange("p (a b) -> p a b", b=HD),
                expuc[:, h0h:h0h + nheads].unsqueeze(2).broadcast_to(
                    [128, nheads, HD]), ALU.mult),
                waits=[(SE_T, stv)], dr=True)
            vop(lambda e, c=c, f0=f0: e.tensor_add(
                ynat(c)[:, f0:f0 + QW], stg[:, 0, 0:QW], PYA[:, 0:QW]), dr=True)
        dh_update(c, hcur, hnew, gv)
    s_y = P.val(SE_V)
    if getattr(cfg, "dbg", False):
        dma("scalar", dbg_y[:, :], poolC[:, 0:NCH * DS], waits=[(SE_V, s_y)])

    if getattr(cfg, "stop_after", 99) < 7:
        return _finish(nc, P)
    NOT_ = D // 128
    for o in range(NOT_):
        b = o % 2
        wts = [(SE_V, P.val(SE_V))] if o >= 2 else [(SE_V, s_y)]
        vv = dma("sync", fbig[b][:, 0:D], ffw[o * 128:(o + 1) * 128, :], waits=wts)
        vop(lambda e, b=b, o=o: e.reduce_max(
            amx[:, 0:1] if o == 0 else amx[:, 1:2], fbig[b][:, 0:D],
            axis=AX.X, apply_absolute_value=True),
            waits=[(SQ_W, vv)], dr=True)
        if o > 0:
            vop(lambda e: e.tensor_max(amx[:, 0:1], amx[:, 0:1], amx[:, 1:2]),
                dr=True)
    sv = P.val(SE_V)
    v1 = dma("scalar", rows_scr[6:7, 0:128], amx[:, 0:1], waits=[(SE_V, sv)])
    v2 = dma("gpsimd", rowt[0:1, 1040:1168],
             bass.AP(rows_scr, 6 * RW, [[0, 1], [1, 128]]), waits=[(SQ_S, v1)])
    vop(lambda e: e.reduce_max(rowt[0:1, 1168:1169], rowt[0:1, 1040:1168], axis=AX.X),
        waits=[(SQ_G, v2)], dr=True)
    vop(lambda e: e.reciprocal(rowt[0:1, 1169:1170], rowt[0:1, 1168:1169]), dr=True)
    sv = vop(lambda e: e.tensor_scalar_mul(rowt[0:1, 1170:1171], rowt[0:1, 1169:1170],
                                           224.0), dr=True)
    v1 = dma("scalar", rows_scr[7:8, 0:3], rowt[0:1, 1168:1171], waits=[(SE_V, sv)])
    vqs = dma("gpsimd", amx[:, 2:3],
              bass.AP(rows_scr, 7 * RW + 2, [[0, 128], [1, 1]]),
              waits=[(SQ_S, v1)])
    wq8_store = {}
    for o in range(NOT_):
        b = o % 2
        wts = [(SE_V, P.val(SE_V))]
        vv = dma("sync", fbig[b][:, 0:D], ffw[o * 128:(o + 1) * 128, :], waits=wts)
        sv = vop(lambda e, b=b: e.tensor_scalar_mul(
            wq8[:, 0, :], fbig[b][:, 0:D], amx[:, 2:3]),
            waits=[(SQ_W, vv), (SQ_G, vqs), (SQ_G, wq8_store.get(o - 1, 0))])
        wq8_store[o] = dma("gpsimd", wqbf[o * 128:(o + 1) * 128, :], wq8[:, 0, :],
                           waits=[(SE_V, sv)])
    vwq_done = P.val(SQ_G)

    s_tr_evac = 0
    tr_evac = {}
    for ft in range(KT):
        b = ft % 2
        vz = dma("sync", zst[:, b, 0:T], z_scr[ft, :, :],
                 waits=[(SE_V, tr_evac.get(ft - 2, 0)), (SQ_S, vz_done)])
        ss = sop(lambda e, b=b: e.activation(zst[:, b, 0:T], zst[:, b, 0:T],
                                             ACT.Silu),
                 waits=[(SQ_W, vz), (SE_V, s_y)])
        for c in range(NCH):
            slot = c % 8
            wts = [(SE_S, ss)] if c == 0 else [(SE_V, P.val(SE_V))]
            if c == 0 and ft > 0:
                wts.append((SE_V, s_tr_evac))
            stv = top(lambda e, c=c, ft=ft, slot=slot: e.matmul(
                PTB[:, slot * 128:(slot + 1) * 128],
                ynat(c)[:, ft * 128:(ft + 1) * 128], idbf[:, :],
                is_transpose=True), waits=wts)
            vop(lambda e, c=c, ft=ft, slot=slot, b=b: e.tensor_mul(
                rhs3_slab(ft, c)[:, 0:128], PTB[:, slot * 128:(slot + 1) * 128],
                zst[:, b, c * 128:(c + 1) * 128]),
                waits=[(SE_T, stv)])
        s_tr_evac = P.val(SE_V)
        tr_evac[ft] = s_tr_evac
    s_yg = P.val(SE_V)
    for st in range(NCH):
        for ft in range(KT):
            sq = ft % 2
            wts = [(SE_V, s_yg)] if ft == 0 else []
            if ft >= 2:
                wts = wts + [(SE_T, P.val(SE_T))]
            ss = sop(lambda e, st=st, ft=ft, sq=sq: e.activation(
                sqt[:, sq, :], rhs3_slab(ft, st)[:, 0:128], ACT.Square),
                waits=wts)
            top(lambda e, ft=ft, sq=sq: e.matmul(
                PA[0:1, 0:128], onec[:, 0:1], sqt[:, sq, :],
                start=(ft == 0), stop=(ft == KT - 1)), waits=[(SE_S, ss)])
        stv = P.val(SE_T)
        sop(lambda e, st=st: e.activation(
            rowt[0:1, st * 128:(st + 1) * 128], PA[0:1, 0:128], ACT.Ln,
            bias=epsc[0:1, 0:1], scale=1.0 / DS),
            waits=[(SE_T, stv), (SE_G, sg_eps)], dr=True)
        sop(lambda e, st=st: e.activation(
            rowt[0:1, st * 128:(st + 1) * 128],
            rowt[0:1, st * 128:(st + 1) * 128], ACT.Exp, scale=-0.5))
    ss = P.val(SE_S)
    v1 = dma("scalar", rows_scr[1:2, 0:T], rowt[0:1, 0:T], waits=[(SE_S, ss)])
    v2 = dma("gpsimd", LBC[:, 0:T], bass.AP(rows_scr, RW, [[0, 128], [1, T]]),
             waits=[(SQ_S, v1)])
    for ft in range(KT):
        for st in range(NCH):
            vop(lambda e, ft=ft, st=st: e.scalar_tensor_tensor(
                rhs3_slab(ft, st)[:, 0:128], LBC[:, st * 128:(st + 1) * 128],
                nw_c[:, ft:ft + 1], rhs3_slab(ft, st)[:, 0:128],
                ALU.mult, ALU.mult),
                waits=[(SQ_G, v2)] if (ft == 0 and st == 0) else [])
    s_rhs3 = P.val(SE_V)
    if getattr(cfg, "dbg", False):
        dma("scalar", dbg_r3[:, :], poolC[:, 0:NCH * DS], waits=[(SE_V, s_rhs3)])

    if getattr(cfg, "stop_after", 99) < 8:
        return _finish(nc, P)
    P.emit("gpsimd", lambda e: e.memset(sumc[:, :, :], 0.0),
           waits=[(SE_V, s_rhs3)], inc=SE_G)
    sgz = P.val(SE_G)
    OM = D // 256
    t_om = []
    v_om = []
    for om in range(OM):
        b = om % 2
        wts = [(SE_T, t_om[om - 2])] if om >= 2 else []
        vv = dma("gpsimd", womb[b],
                 bass.AP(opw, om * 256, [[D, 128], [128 * D, KT], [1, 256]]),
                 waits=wts)
        vg_ = dma("gpsimd", gaB[:, :],
                  bass.AP(rows_scr, 4 * RW + om * 256, [[0, 128], [1, 256]]),
                  waits=[(SE_V, v_om[om - 1])] if om >= 1 else
                  [(SQ_G, g_rows45)])
        for st in range(NCH):
            ps = ps_pair[(om * NCH + st) % 2]
            for ft in range(KT):
                wts = [(SQ_G, vv), (SE_V, s_rhs3)] if ft == 0 else []
                if ft == 0:
                    wts.append((SE_V, P.val(SE_V)))
                top(lambda e, b=b, st=st, ft=ft, ps=ps: e.matmul(
                    ps[:, 0:256], rhs3_slab(ft, st)[:, 0:128],
                    womb[b][:, ft, :],
                    start=(ft == 0), stop=(ft == KT - 1)), waits=wts)
            stv = P.val(SE_T)
            vx = dma("sync", stg[:, 1, (st % 2) * 512:(st % 2) * 512 + 256],
                     x_in[HALO + st * 128:HALO + (st + 1) * 128,
                          om * 256:(om + 1) * 256],
                     waits=[(SE_V, P.val(SE_V))])
            vop(lambda e, ps=ps: e.tensor_tensor(
                stg[:, 0, 0:256], ps[:, 0:256], gaB[:, :], ALU.mult),
                waits=[(SE_T, stv), (SQ_G, vg_), (SQ_W, vx), (SE_G, sgz),
                       (SQ_S, P.val(SQ_S))],
                dr=True)
            vop(lambda e, st=st: e.scalar_tensor_tensor(
                stg[:, 0, 256:512], stg[:, 0, 0:256], 1.0,
                stg[:, 1, (st % 2) * 512:(st % 2) * 512 + 256],
                ALU.mult, ALU.add, accum_out=statc[:, 9:10]), dr=True)
            vop(lambda e: e.scalar_tensor_tensor(
                stg[:, 0, 512:768], stg[:, 0, 256:512], 1.0,
                stg[:, 0, 256:512], ALU.mult, ALU.mult,
                accum_out=statc[:, 10:11]), dr=True)
            vop(lambda e, st=st: e.tensor_add(
                sumc[:, 0, st:st + 1], sumc[:, 0, st:st + 1], statc[:, 9:10]),
                dr=True)
            sv = vop(lambda e, st=st: e.tensor_add(
                sumc[:, 1, st:st + 1], sumc[:, 1, st:st + 1], statc[:, 10:11]),
                dr=True)
            dma("scalar", x2_scr[st * 128:(st + 1) * 128, om * 256:(om + 1) * 256],
                stg[:, 0, 256:512], waits=[(SE_V, sv)])
        t_om.append(P.val(SE_T))
        v_om.append(P.val(SE_V))
    s_x2 = P.val(SE_V)
    vx2 = P.val(SQ_S)
    for st in range(NCH):
        vop(lambda e, st=st: e.tensor_scalar_mul(
            statc[:, 12:13], sumc[:, 0, st:st + 1], 1.0 / D),
            waits=[(SE_V, s_x2)] if st == 0 else [], dr=True)
        vop(lambda e: e.scalar_tensor_tensor(
            statc[:, 13:14], statc[:, 12:13], -1.0, statc[:, 12:13],
            ALU.mult, ALU.mult), dr=True)
        sv = vop(lambda e, st=st: e.scalar_tensor_tensor(
            statc[:, 14:15], sumc[:, 1, st:st + 1], 1.0 / D, statc[:, 13:14],
            ALU.mult, ALU.add), dr=True)
        sop(lambda e: e.activation(statc[:, 15:16], statc[:, 14:15], ACT.Ln,
                                   bias=epsc[:, 0:1]),
            waits=[(SE_V, sv), (SE_G, sg_eps)], dr=True)
        ss = sop(lambda e, st=st: e.activation(
            statc[:, 16 + st:17 + st], statc[:, 15:16], ACT.Exp, scale=-0.5))
        vop(lambda e, st=st: e.scalar_tensor_tensor(
            statc[:, 24 + st:25 + st], statc[:, 12:13], -1.0,
            statc[:, 16 + st:17 + st], ALU.mult, ALU.mult),
            waits=[(SE_S, ss)], dr=True)
    s_ln2 = (P.val(SE_V), P.val(SE_S))
    P.emit("gpsimd", lambda e: e.memset(amx[:, 4:5], 0.0),
           waits=[(SE_V, P.val(SE_V))], inc=SE_G)
    sgz = P.val(SE_G)
    NDC = (D + CBW - 1) // CBW
    for st in range(NCH):
        vx = dma("sync", h2tile[:, 0:D], x2_scr[st * 128:(st + 1) * 128, :],
                 waits=[(SQ_S, vx2)] if st == 0 else
                 [(SQ_S, P.val(SQ_S)), (SE_V, P.val(SE_V))])
        ss = sop(lambda e, st=st: e.activation(
            h2tile[:, 0:D], h2tile[:, 0:D], ACT.Identity,
            scale=statc[:, 16 + st:17 + st], bias=statc[:, 24 + st:25 + st]),
            waits=[(SQ_W, vx), (SE_V, s_ln2[0]), (SE_S, s_ln2[1]),
                   (SE_G, sgz)])
        for dc in range(NDC):
            d0 = dc * CBW
            dl = min(CBW, D - d0)
            r0d, c0d = divmod(d0, SLC)
            gcb = dma("gpsimd", cb1B[:, 0:dl],
                      bass.AP(ag1_out, (r0d * 6 + 4) * SLC + c0d,
                              [[0, 128], [6 * SLC, dl // SLC], [1, SLC]])
                      if dl >= SLC else
                      bass.AP(ag1_out, (r0d * 6 + 4) * SLC + c0d,
                              [[0, 128], [1, dl]]),
                      waits=[(SE_V, P.val(SE_V))])
            gsb = dma("gpsimd", sbB[:, 0:dl],
                      bass.AP(ag1_out, (r0d * 6 + 3) * SLC + c0d,
                              [[0, 128], [6 * SLC, dl // SLC], [1, SLC]])
                      if dl >= SLC else
                      bass.AP(ag1_out, (r0d * 6 + 3) * SLC + c0d,
                              [[0, 128], [1, dl]]))
            vop(lambda e, dl=dl: e.tensor_scalar_add(cb1B[:, 0:dl],
                                                     cb1B[:, 0:dl], 1.0),
                waits=[(SQ_G, gsb), (SE_S, ss)], dr=True)
            vop(lambda e, d0=d0, dl=dl: e.tensor_mul(
                h2tile[:, d0:d0 + dl], h2tile[:, d0:d0 + dl], cb1B[:, 0:dl]),
                dr=True)
            vop(lambda e, d0=d0, dl=dl: e.tensor_add(
                h2tile[:, d0:d0 + dl], h2tile[:, d0:d0 + dl], sbB[:, 0:dl]),
                dr=True)
        vop(lambda e: e.reduce_max(statc[:, 11:12], h2tile[:, 0:D], axis=AX.X,
                                   apply_absolute_value=True), dr=True)
        sv = vop(lambda e: e.tensor_max(amx[:, 4:5], amx[:, 4:5],
                                        statc[:, 11:12]), dr=True)
        dma("scalar", h2_scr[st * 128:(st + 1) * 128, :], h2tile[:, 0:D],
            waits=[(SE_V, sv)])
    vh2 = P.val(SQ_S)
    sv = P.val(SE_V)
    v1 = dma("scalar", rows_scr[6:7, 0:128], amx[:, 4:5], waits=[(SE_V, sv)])
    v2 = dma("gpsimd", rowt[0:1, 1040:1168],
             bass.AP(rows_scr, 6 * RW, [[0, 1], [1, 128]]), waits=[(SQ_S, v1)])
    sv = vop(lambda e: e.reduce_max(rowt[0:1, 1172:1173], rowt[0:1, 1040:1168],
                                    axis=AX.X), waits=[(SQ_G, v2)], dr=True)
    P.emit("gpsimd", lambda e: e.memset(rowt[0:1, 1330:1346], 0.0), inc=SE_G)
    sgz3 = P.val(SE_G)
    dma("scalar", ag3_in[0:1, 0:16], rowt[0:1, 1330:1346], waits=[(SE_G, sgz3)])
    v1 = dma("scalar", ag3_in[0:1, 0:1], rowt[0:1, 1172:1173], waits=[(SE_V, sv)])
    P.emit("gpsimd",
           lambda e: e.collective_compute(
               "AllGather", ALU.bypass, replica_groups=[list(range(NC))],
               ins=[ag3_in[:, :].opt()], outs=[ag3_out[:, :].opt()]),
           waits=[(SQ_S, v1)], inc=SE_C)
    agc3 = P.val(SE_C)
    v2 = dma("gpsimd", rowt[0:1, 1196:1196 + 16 * NC],
             bass.AP(ag3_out, 0, [[0, 1], [1, 16 * NC]]), waits=[(SE_C, agc3)])
    vop(lambda e: e.reduce_max(
        rowt[0:1, 1173:1174],
        bass.AP(rowt, 1196, [[RWT, 1], [16, NC]]), axis=AX.X),
        waits=[(SQ_G, v2)], dr=True)
    vop(lambda e: e.reciprocal(rowt[0:1, 1174:1175], rowt[0:1, 1173:1174]), dr=True)
    vop(lambda e: e.tensor_scalar_mul(rowt[0:1, 1175:1176], rowt[0:1, 1174:1175],
                                      224.0), dr=True)
    v3 = dma("gpsimd", rowt[0:1, 1350:1351], bass.AP(rows_scr, 7 * RW, [[0, 1], [1, 1]]))
    vop(lambda e: e.tensor_mul(rowt[0:1, 1176:1177], rowt[0:1, 1173:1174],
                               rowt[0:1, 1350:1351]), waits=[(SQ_G, v3)], dr=True)
    sv = vop(lambda e: e.tensor_scalar_mul(
        rowt[0:1, 1177:1178], rowt[0:1, 1176:1177], 4.0 / (448.0 * 448.0)), dr=True)
    v1 = dma("scalar", rows_scr[7:8, 4:8], rowt[0:1, 1174:1178], waits=[(SE_V, sv)])
    dma("gpsimd", amx[:, 5:6], bass.AP(rows_scr, 7 * RW + 5, [[0, 128], [1, 1]]),
        waits=[(SQ_S, v1)])
    g_scales = dma("gpsimd", amx[:, 6:7],
                   bass.AP(rows_scr, 7 * RW + 7, [[0, 128], [1, 1]]))

    if getattr(cfg, "stop_after", 99) < 10:
        return _finish(nc, P)
    for st in range(NCH):
        b = st % 2
        vx = dma("sync", fbig[b][:, 0:D], h2_scr[st * 128:(st + 1) * 128, :],
                 waits=[(SQ_S, vh2)] if st == 0 else [(SE_V, P.val(SE_V))])
        sv = vop(lambda e, b=b: e.tensor_scalar_mul(
            h2q8[:, 0, :], fbig[b][:, 0:D], amx[:, 5:6]),
            waits=[(SQ_W, vx), (SQ_G, g_scales), (SQ_G, P.val(SQ_G))])
        dma("gpsimd", h2bf[st * 128:(st + 1) * 128, :], h2q8[:, 0, :],
            waits=[(SE_V, sv)])
    vhbf = P.val(SQ_G)
    for kt in range(KT):
        dma("scalar", h2qT(kt)[:, 0:T], h2bf[0:T, kt * 128:(kt + 1) * 128],
            waits=[(SQ_G, vhbf), (SE_V, P.val(SE_V))] if kt == 0 else [],
            transpose=True)
    vh2t = P.val(SQ_S)
    t_om, v_om = [], []
    for om in range(OM):
        b = om % 2
        for kt in range(KT):
            wts = []
            if kt == 0:
                wts = [(SQ_S, vh2t), (SQ_G, vwq_done)]
                if om >= 2:
                    wts.append((SE_T, t_om[om - 2]))
            dma("scalar", womb[b][:, kt, :],
                wqbf[om * 256:(om + 1) * 256, kt * 128:(kt + 1) * 128],
                waits=wts, transpose=True)
        vwq = P.val(SQ_S)
        vg1 = dma("gpsimd", gbB[:, :],
                  bass.AP(rows_scr, 5 * RW + om * 256, [[0, 128], [1, 256]]),
                  waits=[(SE_V, v_om[om - 1])] if om >= 1 else [])
        vg2 = dma("gpsimd", bofs[:, :],
                  vsl("ffnb", om * 256, (om + 1) * 256).broadcast_to([128, 256]))
        for st in range(NCH):
            ps = ps_pair[(om * NCH + st) % 2]
            for kt in range(KT):
                wts = [(SQ_S, vwq)] if kt == 0 else []
                if kt == 0:
                    wts.append((SE_V, P.val(SE_V)))
                top(lambda e, b=b, st=st, kt=kt, ps=ps: e.matmul(
                    ps[:, 0:256], h2qT(kt)[:, st * 128:(st + 1) * 128],
                    womb[b][:, kt, :], start=(kt == 0), stop=(kt == KT - 1)),
                    waits=wts)
            stv = P.val(SE_T)
            vx = dma("sync", stg[:, 1, (st % 2) * 512:(st % 2) * 512 + 256],
                     x2_scr[st * 128:(st + 1) * 128, om * 256:(om + 1) * 256],
                     waits=[(SE_V, P.val(SE_V))])
            vop(lambda e, ps=ps: e.tensor_scalar_mul(
                stg[:, 0, 0:256], ps[:, 0:256], amx[:, 6:7]),
                waits=[(SE_T, stv), (SQ_G, vg2), (SQ_W, vx),
                       (SQ_S, P.val(SQ_S))], dr=True)
            vop(lambda e: e.tensor_add(stg[:, 0, 0:256], stg[:, 0, 0:256],
                                       bofs[:, :]), dr=True)
            vop(lambda e: e.tensor_mul(stg[:, 0, 0:256], stg[:, 0, 0:256],
                                       gbB[:, :]), dr=True)
            sv = vop(lambda e, st=st: e.tensor_add(
                stg[:, 0, 256:512], stg[:, 0, 0:256],
                stg[:, 1, (st % 2) * 512:(st % 2) * 512 + 256]), dr=True)
            dma("scalar", out[st * 128:(st + 1) * 128, om * 256:(om + 1) * 256],
                stg[:, 0, 256:512], waits=[(SE_V, sv)])
        t_om.append(P.val(SE_T))
        v_om.append(P.val(SE_V))
    v_out = P.val(SQ_S)
    P.wait("gpsimd", [(SQ_S, v_out)])
    P.wait("sync", [(SQ_S, v_out)])
    P.wait("tensor", [(SQ_S, v_out)])

    return _finish(nc, P)


def _finish(nc, P):
    import contextlib
    with contextlib.ExitStack() as es:
        for s in ALL_SEMS:
            P.sems[s] = es.enter_context(nc.semaphore(s))
        with nc.Block() as block:
            @block.sync
            def _(e):
                for fn in P.ops["sync"]:
                    fn(e)

            @block.gpsimd
            def _(e):
                for fn in P.ops["gpsimd"]:
                    fn(e)

            @block.vector
            def _(e):
                for fn in P.ops["vector"]:
                    fn(e)

            @block.scalar
            def _(e):
                for fn in P.ops["scalar"]:
                    fn(e)

            @block.tensor
            def _(e):
                for fn in P.ops["tensor"]:
                    fn(e)
    return nc


def make_inmaps(cfg, x, ln_w, in_proj_w, conv_w, A_log, dt_bias, Dv, norm_w,
                out_proj_w, mup, ffn_w, ffn_b, cvec, wmod_f):
    x = np.ascontiguousarray(np.asarray(x, np.float32).reshape(-1, cfg.D))
    mup_pad = np.zeros(cfg.OT * 128, np.float32)
    mup_pad[:cfg.PROJ] = np.asarray(mup, np.float32)
    in_maps = []
    for c in range(cfg.NC):
        s0 = c * cfg.T
        xin = np.zeros((cfg.TH, cfg.D), np.float32)
        lo = max(0, s0 - cfg.HALO)
        xin[cfg.HALO - (s0 - lo):] = x[lo:s0 + cfg.T]
        halo = np.full(cfg.HALO, 0.0 if c == 0 else 1.0, np.float32)
        segm = (np.arange(cfg.NC) < c).astype(np.float32)
        vecs = np.concatenate([
            np.asarray(ln_w, np.float32), np.asarray(norm_w, np.float32),
            mup_pad, np.asarray(ffn_b, np.float32),
            np.asarray(A_log, np.float32), np.asarray(dt_bias, np.float32),
            np.asarray(Dv, np.float32),
            np.asarray(cvec, np.float32).reshape(-1), halo, segm])[None, :]
        wm = np.ascontiguousarray(np.concatenate(
            [np.asarray(wmod_f, np.float32)
             [:, v * cfg.D + c * cfg.SLC:v * cfg.D + (c + 1) * cfg.SLC]
             for v in range(6)], axis=1))
        in_maps.append({
            "x": np.ascontiguousarray(xin),
            "inproj": np.ascontiguousarray(np.asarray(in_proj_w, np.float32)),
            "outproj": np.ascontiguousarray(np.asarray(out_proj_w, np.float32)),
            "ffnw": np.ascontiguousarray(np.asarray(ffn_w, np.float32)),
            "wmodslc": wm,
            "convw": np.ascontiguousarray(np.asarray(conv_w, np.float32)),
            "vecs": np.ascontiguousarray(vecs),
        })
    return in_maps


def kernel(L0_attn_hidden_states, L0_attn_residual, L0_attn_ln_w,
           L0_attn_in_proj_w, L0_attn_conv_w, L0_attn_A_log, L0_attn_dt_bias,
           L0_attn_D, L0_attn_norm_w, L0_attn_out_proj_w, L0_attn_mup_vector,
           L0_ffn_weight, L0_ffn_bias, L0_c, L0_wmod):
    cfg = Cfg(D=4096, NH=64, T=1024, NC=8)
    nc_prog = build(cfg)
    in_maps = make_inmaps(
        cfg, L0_attn_hidden_states, L0_attn_ln_w, L0_attn_in_proj_w,
        L0_attn_conv_w, L0_attn_A_log, L0_attn_dt_bias, L0_attn_D,
        L0_attn_norm_w, L0_attn_out_proj_w, L0_attn_mup_vector,
        L0_ffn_weight, L0_ffn_bias, L0_c, L0_wmod)
    res = run_bass_kernel_spmd(nc_prog, in_maps, core_ids=list(range(cfg.NC)))
    outs = [np.asarray(r["out"]) for r in res.results]
    return np.concatenate(outs, axis=0).reshape(1, 8192, 4096).astype(np.float32)


# version 06
# speedup vs baseline: 1.1065x; 1.1065x over previous
import numpy as np
import ml_dtypes

import concourse.bass as bass
import concourse.mybir as mybir
from concourse.bass_utils import run_bass_kernel_spmd

F32 = mybir.dt.float32
BF16 = mybir.dt.bfloat16
FP8 = mybir.dt.float8e4
ALU = mybir.AluOpType
ACT = mybir.ActivationFunctionType
AX = mybir.AxisListType
BF = ml_dtypes.bfloat16

ENGS = ("sync", "gpsimd", "vector", "scalar", "tensor")
SQ_W, SQ_S, SQ_G, SQ_V = "sqw", "sqs", "sqg", "sqv"
SE_T, SE_V, SE_S, SE_G, SE_C = "st", "sv", "ss", "sg", "scc"
RQ = 4
QUEUE_SEMS = {q: tuple(f"{q}{i}" for i in range(RQ))
              for q in (SQ_W, SQ_S, SQ_G, SQ_V)}
ALL_SEMS = tuple(s for ss in QUEUE_SEMS.values() for s in ss) + (
    SE_T, SE_V, SE_S, SE_G, SE_C)


class Cfg:
    def __init__(self, D=4096, NH=64, T=1024, NC=8):
        self.D, self.NH, self.T, self.NC = D, NH, T, NC
        self.HD, self.S, self.Q, self.HALO = 64, 128, 128, 3
        self.DS = NH * self.HD
        assert self.DS == D
        self.CONV = self.DS + 2 * self.S
        self.PROJ = 2 * self.DS + 2 * self.S + NH
        self.KT = D // 128
        self.ZT = self.DS // 128
        self.XT = self.CONV // 128
        self.OT = self.ZT + self.XT + 1
        self.NCH = T // self.Q
        self.TH = T + self.HALO
        self.SLC = 6 * D // NC // 6
        self.MODW = 6 * self.SLC
        assert self.SLC % 128 == 0


class Prog:
    def __init__(self):
        self.ops = {e: [] for e in ENGS}
        self.cnt = {s: 0 for s in ALL_SEMS}
        self.qn = {q: 0 for q in QUEUE_SEMS}
        self.sems = {}
        self.waited = {e: {} for e in ENGS}

    def inc(self, name, by=1):
        self.cnt[name] += by
        return self.cnt[name]

    def val(self, name):
        if name in QUEUE_SEMS:
            return 16 * self.qn[name]
        return self.cnt[name]

    def expand(self, waits):
        out = []
        for s, v in waits:
            if v <= 0:
                continue
            if s in QUEUE_SEMS:
                n = v // 16
                for i, ps in enumerate(QUEUE_SEMS[s]):
                    c = (n - 1 - i) // RQ + 1 if n - 1 >= i else 0
                    if c > 0:
                        out.append((ps, 16 * c))
            else:
                out.append((s, v))
        return out

    def emit(self, eng, fn, waits=(), inc=None, inc_by=1):
        w = tuple(self.expand(waits))
        post = None
        if inc is not None:
            self.inc(inc, inc_by)
            post = (inc, inc_by)

        def run(engine, _fn=fn, _w=w, _post=post, _eng=eng):
            wd = self.waited[_eng]
            for sname, v in _w:
                if wd.get(sname, 0) < v:
                    engine.wait_ge(self.sems[sname], v)
                    wd[sname] = v
            ins = _fn(engine)
            if _post is not None:
                ins.then_inc(self.sems[_post[0]], _post[1])
            return ins

        self.ops[eng].append(run)

    def wait(self, eng, waits):
        self.emit(eng, lambda e: None, waits=waits)

    def drain(self, eng):
        self.ops[eng].append(lambda e: e.drain())


def build(cfg: Cfg):
    nc = bass.Bass("TRN2", target_bir_lowering=False)
    _ncd = nc.allow_non_contiguous_dma(reason="small scatter/bcast DMAs")
    _ncd.__enter__()
    P = Prog()
    D, NH, HD, S, Q, T = cfg.D, cfg.NH, cfg.HD, cfg.S, cfg.Q, cfg.T
    KT, ZT, XT, OT, NCH, TH = cfg.KT, cfg.ZT, cfg.XT, cfg.OT, cfg.NCH, cfg.TH
    DS, CONV, PROJ, NC, SLC = cfg.DS, cfg.CONV, cfg.PROJ, cfg.NC, cfg.SLC
    HALO = cfg.HALO
    MODW = cfg.MODW

    x_in = nc.dram_tensor("x", [TH, D], F32, kind="ExternalInput")
    ipw = nc.dram_tensor("inproj", [D, PROJ], F32, kind="ExternalInput")
    opw = nc.dram_tensor("outproj", [DS, D], F32, kind="ExternalInput")
    ffw = nc.dram_tensor("ffnw", [D, D], F32, kind="ExternalInput")
    wmod = nc.dram_tensor("wmodslc", [D, MODW], F32, kind="ExternalInput")
    convw_in = nc.dram_tensor("convw", [CONV, 4], F32, kind="ExternalInput")
    VO, off = {}, 0
    for nm, ln in (("lnw", D), ("normw", DS), ("mup", OT * 128), ("ffnb", D),
                   ("alog", NH), ("dtb", NH), ("dvec", NH), ("cvec", D),
                   ("halo", HALO), ("segm", NC)):
        VO[nm] = (off, ln)
        off += ln
    vecs = nc.dram_tensor("vecs", [1, off], F32, kind="ExternalInput")
    out = nc.dram_tensor("out", [T, D], F32, kind="ExternalOutput")

    z_scr = nc.dram_tensor("z_scr", [ZT, 128, T], BF16)
    x2_scr = nc.dram_tensor("x2_scr", [T, D], F32)
    h2_scr = nc.dram_tensor("h2_scr", [T, D], F32)
    h2bf = nc.dram_tensor("h2bf", [T, D], BF16)
    wqbf = nc.dram_tensor("wqbf", [D, D], BF16)
    xnat_scr = nc.dram_tensor("xnat_scr", [T // 128, 128, DS], BF16)
    RW = max(NH * Q, 2 * D, 256)
    rows_scr = nc.dram_tensor("rows_scr", [16, RW], F32)
    ag1_in = nc.dram_tensor("ag1_in", [1, MODW], F32)
    ag1_out = nc.dram_tensor("ag1_out", [NC, MODW], F32)
    ag2_in = nc.dram_tensor("ag2_in", [S + 1, DS], F32)
    ag2_out = nc.dram_tensor("ag2_out", [NC * (S + 1), DS], F32)
    ag3_in = nc.dram_tensor("ag3_in", [1, 16], F32)
    if getattr(cfg, "dbg", False):
        dbg_y = nc.dram_tensor("dbg_y", [128, NCH * DS], BF16)
        dbg_r3 = nc.dram_tensor("dbg_r3", [128, NCH * DS], BF16)
        dbg_hin = nc.dram_tensor("dbg_hin", [128, DS], F32)
    ag3_out = nc.dram_tensor("ag3_out", [NC, 16], F32)

    ident_bf = nc.inline_tensor(np.eye(128, dtype=BF).view(np.uint16), name="idb")
    ident_f = nc.inline_tensor(np.eye(NH, dtype=np.float32), name="idf_")
    ones_c = nc.inline_tensor(np.ones((128, 1), dtype=BF).view(np.uint16), name="on1")
    trin = np.where(np.tril(np.ones((128, 128))).T >= 1, 0.0, -1e30)
    tri_c = nc.inline_tensor(trin.astype(np.float32), name="tri_")

    A = nc.alloc_sbuf_tensor
    poolC = A("poolC", [128, max(KT * (TH + 1), NCH * DS, KT * T,
                                 2 * MODW + 4 * SLC)], BF16)
    poolF = A("poolF", [128, 2 * max(D, 2048)], F32)
    DW = max(D, 2048)
    H0, H1 = poolF[:, 0:D], poolF[:, DW:DW + D]
    LBW = min(8 * Q, 2048)
    LBC = A("lbc", [128, LBW], F32)
    xstage = [poolF[:, 0:D], poolF[:, DW:DW + D]]
    fbig = [poolF[:, 0:D], poolF[:, DW:DW + D]]
    h2tile = poolF[:, 0:D]
    CBW = min(D, 1024)
    cb1B = poolF[:, DW:DW + CBW]
    sbB = poolF[:, DW + CBW:DW + 2 * CBW]

    ring = A("ring", [128, 3, TH], BF16)
    wst = A("wst", [128, 2, KT, 128], BF16)
    KH = KT // 2
    tstg = A("tstg", [128, 2, D], BF16)
    zst = A("zst", [128, 2, T], BF16)
    sqt = A("sqt", [128, 2, 128], BF16)
    stg = A("stg", [128, 2, 1024], F32)
    xtil = A("xtil", [128, DS], BF16)
    hbf = A("hbf", [128, DS], BF16)
    xns = tstg
    xblk = A("xblk", [128, 2, max(T, 1024)], BF16)
    h2q8 = A("h2q8", [128, 1, D], FP8)
    wq8 = h2q8
    wmh = poolC[:, 0:2 * MODW].rearrange("p (b w) -> p b w", b=2)
    wml = poolC[:, 2 * MODW:2 * MODW + 4 * SLC].rearrange("p (b w) -> p b w", b=2)
    wmf = stg

    lnw_c = A("lnw_c", [128, KT], F32)
    nw_c = A("nw_c", [128, KT], F32)
    mup_c = A("mup_c", [128, OT], F32)
    ffnb_c = A("ffnb_c", [128, KT], F32)
    cvec_c = A("cvec_c", [128, KT], F32)
    sc_hi = A("sc_hi", [128, KT], BF16)
    sc_lo = A("sc_lo", [128, KT], BF16)
    convw_c = A("convw_c", [128, XT, 4], F32)
    sa_c = A("sa_c", [128, KT], F32)
    ca1_c = A("ca1_c", [128, KT], F32)
    ga_c = A("ga_c", [128, KT], F32)
    sb_c2 = A("sb_c2", [128, KT], F32)
    cb1_c = A("cb1_c", [128, KT], F32)
    gb_c = A("gb_c", [128, KT], F32)
    dhead = A("dhead", [128, NH], F32)
    halo_c = A("halo_c", [128, HALO], F32)
    segm_t = A("segm_t", [NH, NC], F32)
    idbf = A("idbf", [128, 128], BF16)
    idf = A("idf", [NH, NH], F32)
    onec = A("onec", [128, 1], BF16)
    tri = A("tri", [128, 128], F32)
    drow = A("drow", [NH, 2, T + 2], F32)
    bt = A("bt", [128, T], BF16)
    ct = A("ct", [128, T], BF16)
    bnat = A("bnat", [128, 128], BF16)
    scr64 = A("scr64", [NH, 3 * Q], F32)
    ltc = A("ltc", [128, NH], F32)
    lndtc = A("lndtc", [128, NH], F32)
    expuc = A("expuc", [128, NH], F32)
    alphc = A("alphc", [128, NH], F32)
    gamt = A("gamt", [128, NH], F32)
    etmp = A("etmp", [128, 2, 128], F32)
    ebf = A("ebf", [128, 2, 128], BF16)
    wbf = A("wbf", [128, 4, 128], BF16)
    gbf = A("gbf", [128, 128], BF16)
    arow = A("arow", [NH, 8], F32)
    arow2 = A("arow2", [NH, NC], F32)
    wcomb = A("wcomb", [NH, NC], F32)
    wct = A("wct", [NC, NH], F32)
    hin_t = A("hin_t", [128, NH], F32)
    statc = A("statc", [128, 64], F32)
    sumc = A("sumc", [128, 2, NCH], F32)
    amx = A("amx", [128, 8], F32)
    RWT = max(TH + 1, 1400)
    rowt = A("rowt", [1, RWT], F32)
    gaB = A("gaB", [128, 256], F32)
    gbB = A("gbB", [128, 256], F32)
    bofs = A("bofs", [128, 256], F32)

    womb = [bass.AP(wst, 0, [[2 * KT * 128, 128], [256, KT], [1, 256]]),
            bass.AP(tstg, 0, [[2 * D, 128], [256, KT], [1, 256]])]

    PA = nc.alloc_psum_tensor("PA", [128, 512], F32)
    PB = nc.alloc_psum_tensor("PB", [128, 512], F32)
    PYA = nc.alloc_psum_tensor("PYA", [128, 1024], F32)
    PYB = nc.alloc_psum_tensor("PYB", [128, 1024], F32)
    PDH = nc.alloc_psum_tensor("PDH", [128, 512], F32)
    PTB = nc.alloc_psum_tensor("PTB", [128, 1024], BF16)
    ps_pair = [PA, PB]

    def rhs2_kt(k):
        return poolC[:, k * (TH + 1):k * (TH + 1) + TH]

    def ynat(c):
        return poolC[:, c * DS:(c + 1) * DS]

    def rhs3_slab(ft, st):
        return poolC[:, st * DS + ft * 128: st * DS + ft * 128 + 128]

    def h2qT(k):
        return poolC[:, k * T:(k + 1) * T]

    QKEY = {"sync": SQ_W, "scalar": SQ_S, "gpsimd": SQ_G}

    def dma(q, outap, inap, waits=(), **kw):
        qs = QKEY[q]
        n = P.qn[qs]
        ps = QUEUE_SEMS[qs][n % RQ]
        waits = list(waits) + [(ps, 16 * (n // RQ))]
        P.qn[qs] = n + 1
        P.emit(q, lambda e, o=outap, i=inap, kw=kw: e.dma_start(out=o, in_=i, **kw),
               waits=waits, inc=ps, inc_by=16)
        return 16 * (n + 1)

    def vop(fn, waits=(), dr=False):
        P.emit("vector", fn, waits=waits, inc=SE_V)
        if dr:
            P.drain("vector")
        return P.val(SE_V)

    def sop(fn, waits=(), dr=False):
        P.emit("scalar", fn, waits=waits, inc=SE_S)
        if dr:
            P.drain("scalar")
        return P.val(SE_S)

    def top(fn, waits=()):
        P.emit("tensor", fn, waits=waits, inc=SE_T)
        return P.val(SE_T)

    def bcast_tile(dst, dram_row_ap, waits=()):
        return dma("gpsimd", dst, dram_row_ap, waits=waits)

    def vsl(nm, a=0, b=None):
        o, ln = VO[nm]
        if b is None:
            b = ln
        return vecs[0:1, o + a:o + b]

    epsc = A("epsc", [128, 1], F32)
    P.emit("gpsimd", lambda e: e.memset(epsc[:, :], 1e-5), inc=SE_G)
    sg_eps = P.val(SE_G)

    dma("sync", idbf[:].bitcast(mybir.dt.uint16), ident_bf.ap())
    dma("sync", idf[:], ident_f.ap())
    dma("sync", onec[:].bitcast(mybir.dt.uint16), ones_c.ap())
    dma("sync", tri[:], tri_c.ap())

    def scatter_cols(dst, nm, ntile):
        o, ln = VO[nm]
        dma("gpsimd", dst[:, 0:ntile],
            bass.AP(vecs, o, [[1, 128], [128, ntile]]))

    scatter_cols(lnw_c, "lnw", KT)
    scatter_cols(nw_c, "normw", KT)
    scatter_cols(mup_c, "mup", OT)
    scatter_cols(ffnb_c, "ffnb", KT)
    scatter_cols(cvec_c, "cvec", KT)
    v2 = P.val(SQ_G)
    for k in range(4):
        dma("gpsimd", convw_c[:, :, k],
            bass.AP(convw_in, k, [[4, 128], [512, XT]]))
    dma("gpsimd", arow[0:NH, 0:1], bass.AP(vecs, VO["alog"][0], [[1, NH]]))
    g0 = dma("gpsimd", arow[0:NH, 2:3], bass.AP(vecs, VO["dtb"][0], [[1, NH]]))
    ss = sop(lambda e: e.activation(arow[0:NH, 1:2], arow[0:NH, 0:1], ACT.Exp),
             waits=[(SQ_G, g0)])
    sv = vop(lambda e: e.tensor_scalar_mul(arow[0:NH, 0:1], arow[0:NH, 1:2], -1.0),
             waits=[(SE_S, ss)])
    dma("gpsimd", dhead[:, 0:NH], vsl("dvec").broadcast_to([128, NH]))
    dma("gpsimd", halo_c[:, :], vsl("halo").broadcast_to([128, HALO]))
    g_const = dma("gpsimd", segm_t[0:NH, :], vsl("segm").broadcast_to([NH, NC]))
    s_arow = (sv, ss)

    if getattr(cfg, "stop_after", 99) < 0.5:
        return _finish(nc, P)

    ss = sop(lambda e: e.activation(cvec_c[:, 0:KT], cvec_c[:, 0:KT], ACT.Silu),
             waits=[(SQ_G, v2)])
    vop(lambda e: e.tensor_copy(sc_hi[:, 0:KT], cvec_c[:, 0:KT]),
        waits=[(SE_S, ss)], dr=True)
    sv_sc = vop(lambda e: e.tensor_sub(sc_lo[:, 0:KT], cvec_c[:, 0:KT],
                                       sc_hi[:, 0:KT]), dr=True)
    modps = [PA[0:1, 0:SLC], PB[0:1, 0:SLC], PYA[0:1, 0:SLC],
             PYA[0:1, 512:512 + SLC], PYB[0:1, 0:SLC], PYB[0:1, 512:512 + SLC]]
    GA_V, GB_V = 2, 5
    t_after_k = []
    for k in range(KT):
        b = k % 2
        wts = [(SE_T, t_after_k[k - 2])] if k >= 2 else []
        vh = dma("gpsimd", wmh[:, b, :], wmod[k * 128:(k + 1) * 128, :], waits=wts)
        vf1 = dma("sync", wmf[:, b, 0:SLC],
                  wmod[k * 128:(k + 1) * 128, GA_V * SLC:(GA_V + 1) * SLC], waits=wts)
        vf2 = dma("sync", wmf[:, b, SLC:2 * SLC],
                  wmod[k * 128:(k + 1) * 128, GB_V * SLC:(GB_V + 1) * SLC])
        vop(lambda e, b=b: e.tensor_sub(wml[:, b, 0:SLC], wmf[:, b, 0:SLC],
                                        wmh[:, b, GA_V * SLC:(GA_V + 1) * SLC]),
            waits=[(SQ_G, vh), (SQ_W, vf2)], dr=True)
        sv = vop(lambda e, b=b: e.tensor_sub(wml[:, b, SLC:2 * SLC],
                                             wmf[:, b, SLC:2 * SLC],
                                             wmh[:, b, GB_V * SLC:(GB_V + 1) * SLC]),
                 dr=True)
        for v6 in range(6):
            top(lambda e, k=k, b=b, v6=v6: e.matmul(
                modps[v6], sc_hi[:, k:k + 1], wmh[:, b, v6 * SLC:(v6 + 1) * SLC],
                start=(k == 0), stop=(k == KT - 1 and v6 not in (GA_V, GB_V))),
                waits=([(SE_V, sv)] if v6 == 0 else []))
        for v6, lo in ((GA_V, 0), (GB_V, SLC)):
            top(lambda e, k=k, b=b, v6=v6: e.matmul(
                modps[v6], sc_lo[:, k:k + 1], wmh[:, b, v6 * SLC:(v6 + 1) * SLC],
                start=False, stop=False))
            top(lambda e, k=k, b=b, v6=v6, lo=lo: e.matmul(
                modps[v6], sc_hi[:, k:k + 1], wml[:, b, lo:lo + SLC],
                start=False, stop=(k == KT - 1)))
        t_after_k.append(P.val(SE_T))
    st_mod = P.val(SE_T)
    mod_dma = {}
    for v6 in range(6):
        slot = (v6 % 2) * 512
        wts = [(SE_T, st_mod)] if v6 == 0 else []
        if v6 >= 2:
            wts.append((SQ_G, mod_dma[v6 - 2]))
        sv = vop(lambda e, v6=v6, slot=slot: e.tensor_copy(
            rowt[0:1, slot:slot + SLC], modps[v6]), waits=wts)
        mod_dma[v6] = dma("gpsimd", ag1_in[0:1, v6 * SLC:(v6 + 1) * SLC],
                          rowt[0:1, slot:slot + SLC], waits=[(SE_V, sv)])
    v_ag1 = mod_dma[5]

    if getattr(cfg, "stop_after", 99) < 1.0:
        return _finish(nc, P)
    P.emit("gpsimd",
           lambda e: e.collective_compute(
               "AllGather", ALU.bypass, replica_groups=[list(range(NC))],
               ins=[ag1_in[:, :].opt()], outs=[ag1_out[:, :].opt()]),
           waits=[(SQ_G, v_ag1)], inc=SE_C)
    agc1 = P.val(SE_C)
    spt = SLC // 128
    for v6, dst in ((0, sa_c), (1, ca1_c), (2, ga_c), (3, sb_c2), (4, cb1_c),
                    (5, gb_c)):
        for r in range(NC):
            dma("gpsimd", dst[:, r * spt:(r + 1) * spt],
                bass.AP(ag1_out, (r * 6 + v6) * SLC, [[1, 128], [128, spt]]),
                waits=[(SE_C, agc1)])
    g_mod = P.val(SQ_G)

    if getattr(cfg, "stop_after", 99) < 1.5:
        return _finish(nc, P)
    vop(lambda e: e.tensor_scalar_add(ca1_c[:, 0:KT], ca1_c[:, 0:KT], 1.0),
        waits=[(SQ_G, g_mod)], dr=True)
    vop(lambda e: e.tensor_scalar_add(cb1_c[:, 0:KT], cb1_c[:, 0:KT], 1.0), dr=True)
    for g in (ga_c, gb_c):
        sv = vop(lambda e, g=g: e.tensor_scalar(statc[:, 0:KT], g[:, 0:KT], 0.0,
                                                None, ALU.is_ge), dr=True)
        vop(lambda e: e.tensor_scalar(statc[:, 32:32 + KT], statc[:, 0:KT],
                                      2.0, 1.0, ALU.mult, ALU.subtract), dr=True)
        ssx = sop(lambda e, g=g: e.activation(g[:, 0:KT], g[:, 0:KT], ACT.Abs),
                  waits=[(SE_V, sv)])
        vop(lambda e, g=g: e.tensor_scalar_max(g[:, 0:KT], g[:, 0:KT], 0.25),
            waits=[(SE_S, ssx)], dr=True)
        vop(lambda e, g=g: e.tensor_mul(g[:, 0:KT], g[:, 0:KT],
                                        statc[:, 32:32 + KT]), dr=True)
    s_modcols = P.val(SE_V)
    for ri, g in ((4, ga_c), (5, gb_c)):
        dma("gpsimd", bass.AP(rows_scr, ri * RW, [[0, 1], [1, 128], [128, KT]]),
            g[:, 0:KT], waits=[(SE_V, s_modcols)])
    g_rows45 = P.val(SQ_G)

    if getattr(cfg, "stop_after", 99) < 2:
        return _finish(nc, P)
    groups = [(0, HALO)] + [(HALO + i * 128, 128) for i in range(NCH)]
    s_use = {0: 0, 1: 0}
    NDCH = (D + T - 1) // T
    for gi, (r0, nr) in enumerate(groups):
        b = gi % 2
        wl = [(SE_S, s_use[b])] if gi >= 2 else []
        vx = dma("sync", xstage[b][0:nr, :], x_in[r0:r0 + nr, :], waits=wl)
        w0 = [(SQ_W, vx), (SE_V, s_modcols)]
        vop(lambda e, b=b, nr=nr: e.reduce_sum(statc[0:nr, 0:1],
                                               xstage[b][0:nr, :], axis=AX.X),
            waits=w0, dr=True)
        for j in range(NDCH):
            c0, c1 = j * T, min(D, (j + 1) * T)
            vop(lambda e, b=b, nr=nr, c0=c0, c1=c1, j=j: e.scalar_tensor_tensor(
                stg[:, 0, 0:c1 - c0][0:nr, :], xstage[b][0:nr, c0:c1], 1.0,
                xstage[b][0:nr, c0:c1], ALU.mult, ALU.mult,
                accum_out=statc[0:nr, 1 + j:2 + j]),
                dr=True)
        vop(lambda e, nr=nr: e.reduce_sum(statc[0:nr, 8:9],
                                          statc[0:nr, 1:1 + NDCH], axis=AX.X),
            dr=True)
        vop(lambda e, nr=nr: e.tensor_scalar_mul(statc[0:nr, 9:10],
                                                 statc[0:nr, 0:1], 1.0 / D), dr=True)
        vop(lambda e, nr=nr: e.scalar_tensor_tensor(
            statc[0:nr, 10:11], statc[0:nr, 9:10], -1.0, statc[0:nr, 9:10],
            ALU.mult, ALU.mult), dr=True)
        sv = vop(lambda e, nr=nr: e.scalar_tensor_tensor(
            statc[0:nr, 11:12], statc[0:nr, 8:9], 1.0 / D, statc[0:nr, 10:11],
            ALU.mult, ALU.add), dr=True)
        sop(lambda e, nr=nr: e.activation(statc[0:nr, 12:13], statc[0:nr, 11:12],
                                          ACT.Ln, bias=epsc[0:nr, 0:1]),
            waits=[(SE_V, sv), (SE_G, sg_eps)], dr=True)
        ss = sop(lambda e, nr=nr: e.activation(statc[0:nr, 13:14],
                                               statc[0:nr, 12:13],
                                               ACT.Exp, scale=-0.5), dr=True)
        sv = vop(lambda e, nr=nr: e.scalar_tensor_tensor(
            statc[0:nr, 14:15], statc[0:nr, 9:10], -1.0, statc[0:nr, 13:14],
            ALU.mult, ALU.mult), waits=[(SE_S, ss)], dr=True)
        if gi == 1 and getattr(cfg, "sub", 99) < 1:
            return _finish(nc, P)
        ss = sop(lambda e, b=b, nr=nr: e.activation(
            tstg[:, b, :][0:nr, :], xstage[b][0:nr, :], ACT.Identity,
            bias=statc[0:nr, 14:15], scale=statc[0:nr, 13:14]),
            waits=[(SE_V, sv)])
        s_use[b] = ss
        if gi == 1 and getattr(cfg, "sub", 99) < 2:
            return _finish(nc, P)
        sv_prev_evac = 0
        for kt in range(KT):
            if gi == 1 and kt > getattr(cfg, "sub", 99):
                break
            slot = kt % 8
            wts = [(SE_S, ss)] if kt == 0 else []
            if kt >= 1:
                wts = wts + [(SE_S, P.val(SE_S))]
            stv = top(lambda e, b=b, kt=kt, nr=nr, slot=slot: e.matmul(
                PTB[:, slot * 128:slot * 128 + nr],
                tstg[:, b, :][0:nr, kt * 128:(kt + 1) * 128],
                idbf[0:nr, 0:nr], is_transpose=True), waits=wts)
            sop(lambda e, kt=kt, r0=r0, nr=nr, slot=slot: e.activation(
                rhs2_kt(kt)[:, r0:r0 + nr], PTB[:, slot * 128:slot * 128 + nr],
                ACT.Identity, bias=sa_c[:, kt:kt + 1], scale=ca1_c[:, kt:kt + 1]),
                waits=[(SE_T, stv)])
        s_use[b] = P.val(SE_S)
        if getattr(cfg, "stop_after", 99) < 2.05 + gi * 0.01:
            return _finish(nc, P)
    s_h1 = P.val(SE_S)
    if getattr(cfg, "stop_after", 99) < 2.2:
        return _finish(nc, P)
    CW2 = min(512, T)
    ncc = (TH + CW2 - 1) // CW2
    for cc in range(ncc):
        c0, c1 = cc * CW2, min(TH, cc * CW2 + CW2)
        for kt in range(KT):
            sq = kt % 2
            wts = [(SE_S, s_h1)] if (cc == 0 and kt == 0) else []
            if kt >= 2:
                wts = wts + [(SE_T, P.val(SE_T))]
            ss = sop(lambda e, kt=kt, c0=c0, c1=c1, sq=sq: e.activation(
                zst[:, sq, 0:c1 - c0], rhs2_kt(kt)[:, c0:c1], ACT.Square),
                waits=wts)
            top(lambda e, c0=c0, c1=c1, sq=sq, kt=kt: e.matmul(
                PA[0:1, 0:c1 - c0], onec[:, 0:1], zst[:, sq, 0:c1 - c0],
                start=(kt == 0), stop=(kt == KT - 1)),
                waits=[(SE_S, ss)])
        stv = P.val(SE_T)
        sop(lambda e, c0=c0, c1=c1: e.activation(
            rowt[0:1, c0:c1], PA[0:1, 0:c1 - c0], ACT.Ln,
            bias=epsc[0:1, 0:1], scale=1.0 / DS),
            waits=[(SE_T, stv), (SE_G, sg_eps), (SQ_G, v_ag1)], dr=True)
        sop(lambda e, c0=c0, c1=c1: e.activation(
            rowt[0:1, c0:c1], rowt[0:1, c0:c1], ACT.Exp, scale=-0.5))
    ss = P.val(SE_S)
    if getattr(cfg, "stop_after", 99) < 2.4:
        return _finish(nc, P)
    v1 = dma("scalar", rows_scr[0:1, 0:TH], rowt[0:1, 0:TH], waits=[(SE_S, ss)])
    for cc0 in range(0, TH, LBW):
        cl = min(LBW, TH - cc0)
        v2 = dma("gpsimd", LBC[:, 0:cl],
                 bass.AP(rows_scr, cc0, [[0, 128], [1, cl]]),
                 waits=[(SQ_S, v1), (SE_V, P.val(SE_V))])
        for kt in range(KT):
            vop(lambda e, kt=kt, cc0=cc0, cl=cl: e.scalar_tensor_tensor(
                rhs2_kt(kt)[:, cc0:cc0 + cl], LBC[:, 0:cl], lnw_c[:, kt:kt + 1],
                rhs2_kt(kt)[:, cc0:cc0 + cl], ALU.mult, ALU.mult),
                waits=[(SQ_G, v2), (SE_S, ss)] if kt == 0 else [], dr=True)
    for kt in range(KT):
        vop(lambda e, kt=kt: e.tensor_mul(
            rhs2_kt(kt)[:, 0:HALO], rhs2_kt(kt)[:, 0:HALO], halo_c[:, 0:HALO]),
            dr=True)
    s_rhs2 = P.val(SE_V)

    if getattr(cfg, "stop_after", 99) < 3:
        return _finish(nc, P)
    if T % 512 == 0:
        CH = [(0, HALO)] + [(HALO + i * 512, 512) for i in range(T // 512)]
    else:
        CH = [(0, HALO)] + [(HALO + i * 256, 256) for i in range(T // 256)]
    order = list(range(ZT, ZT + XT)) + [OT - 1] + list(range(ZT))
    otw = {ot: (NH if ot == OT - 1 else 128) for ot in range(OT)}
    z_store_after = {}
    xblk_store = {}
    pscnt = [0]
    bank_evac = {0: 0, 1: 0}
    s_conv_after = {}
    s_xnat_after = {}
    t_gemm_after = []
    v_evac_after = []
    for oi, ot in enumerate(order):
        b = oi % 2
        wts = [(SE_S, v_evac_after[oi - 2][0]), (SE_V, v_evac_after[oi - 2][1])] \
            if oi >= 2 else []
        wv = dma("gpsimd", wst[:, b, :, 0:otw[ot]],
                 bass.AP(ipw, ot * 128,
                         [[PROJ, 128], [128 * PROJ, KT], [1, otw[ot]]]),
                 waits=[(SE_T, t_gemm_after[oi - 2])] if oi >= 2 else [])
        is_x = ZT <= ot < ZT + XT
        is_dt = ot == OT - 1
        t = ot - ZT
        chs = CH if is_x else CH[1:]
        for ci, (c0, cl) in enumerate(chs):
            pb = pscnt[0] % 2
            pscnt[0] += 1
            ps = ps_pair[pb]
            for kt in range(KT):
                wts = []
                if kt == 0:
                    wts = [(SQ_G, wv), (SE_V, s_rhs2),
                           (SE_S, bank_evac[pb])]
                top(lambda e, b=b, kt=kt, ot=ot, c0=c0, cl=cl, ps=ps: e.matmul(
                    ps[0:otw[ot], 0:cl], wst[:, b, kt, 0:otw[ot]],
                    rhs2_kt(kt)[:, c0:c0 + cl],
                    start=(kt == 0), stop=(kt == KT - 1)), waits=wts)
            stv = P.val(SE_T)
            if is_x:
                wts = [(SE_T, stv)]
                if t >= 3 and ci == 0:
                    wts.append((SE_V, s_conv_after.get(t - 3, 0)))
                sop(lambda e, t=t, ot=ot, c0=c0, cl=cl, ps=ps: e.activation(
                    ring[:, t % 3, c0:c0 + cl], ps[:, 0:cl], ACT.Identity,
                    scale=mup_c[:, ot:ot + 1]), waits=wts)
                bank_evac[pb] = P.val(SE_S)
            elif is_dt:
                sop(lambda e, c0=c0, cl=cl, ps=ps: e.activation(
                    stg[0:NH, 1, 0:cl], ps[0:NH, 0:cl], ACT.Exp,
                    scale=mup_c[0:NH, OT - 1:OT], bias=arow[0:NH, 2:3]),
                    waits=[(SE_T, stv), (SE_S, s_arow[1]), (SE_V, s_arow[0])],
                    dr=True)
                sop(lambda e, c0=c0, cl=cl: e.activation(
                    drow[0:NH, 0, c0 - HALO:c0 - HALO + cl],
                    stg[0:NH, 1, 0:cl], ACT.Ln, bias=1.0))
                bank_evac[pb] = P.val(SE_S)
            else:
                wts = [(SE_T, stv), (SE_V, P.val(SE_V))]
                if ci == 0 and z_store_after.get(oi - 2) is not None:
                    wts.append((SQ_S, z_store_after[oi - 2]))
                sop(lambda e, ot=ot, c0=c0, cl=cl, ps=ps, b=b: e.activation(
                    zst[:, b, c0 - HALO:c0 - HALO + cl], ps[:, 0:cl], ACT.Identity,
                    scale=mup_c[:, ot:ot + 1]), waits=wts)
                bank_evac[pb] = P.val(SE_S)
        v_evac_after.append((P.val(SE_S), P.val(SE_V)))
        t_gemm_after.append(P.val(SE_T))
        if is_x:
            ss = P.val(SE_S)
            vop(lambda e, t=t: e.tensor_scalar_mul(
                stg[:, 0, 0:T], ring[:, t % 3, 0:T], convw_c[:, t, 0:1]),
                waits=[(SE_S, ss)], dr=True)
            for k in range(1, 4):
                vop(lambda e, t=t, k=k: e.scalar_tensor_tensor(
                    stg[:, 0, 0:T], ring[:, t % 3, k:k + T],
                    convw_c[:, t, k:k + 1], stg[:, 0, 0:T], ALU.mult, ALU.add),
                    dr=True)
            sv = P.val(SE_V)
            s_conv_after[t] = sv
            if t == XT - 2:
                sop(lambda e: e.activation(bt[:, 0:T], stg[:, 0, 0:T], ACT.Silu),
                    waits=[(SE_V, sv)])
            elif t == XT - 1:
                sop(lambda e: e.activation(ct[:, 0:T], stg[:, 0, 0:T], ACT.Silu),
                    waits=[(SE_V, sv)])
            else:
                wts = [(SE_V, sv)]
                if t >= 2:
                    wts.append((SE_V, s_xnat_after.get(t - 2, 0)))
                ss2 = sop(lambda e, b=b: e.activation(
                    zst[:, b, 0:T], stg[:, 0, 0:T], ACT.Silu), waits=wts)
                for c in range(NCH):
                    slot = c % 8
                    wts = [(SE_S, ss2)] if c == 0 else []
                    if c == 0:
                        wts.append((SQ_S, xblk_store.get(t - 2, 0)))
                    if c >= 1:
                        wts = wts + [(SE_V, P.val(SE_V))]
                    stv = top(lambda e, b=b, c=c, slot=slot: e.matmul(
                        PTB[:, slot * 128:(slot + 1) * 128],
                        zst[:, b, c * 128:(c + 1) * 128], idbf[:, :],
                        is_transpose=True), waits=wts)
                    vop(lambda e, c=c, t=t, slot=slot: e.tensor_copy(
                        xblk[:, t % 2, c * 128:(c + 1) * 128],
                        PTB[:, slot * 128:(slot + 1) * 128]),
                        waits=[(SE_T, stv)])
                s_xnat_after[t] = P.val(SE_V)
                xblk_store[t] = dma(
                    "scalar",
                    bass.AP(xnat_scr, t * 128,
                            [[DS, 128], [128 * DS, NCH], [1, 128]]),
                    xblk[:, t % 2, 0:NCH * 128].rearrange(
                        "p (c q) -> p c q", q=128),
                    waits=[(SE_V, s_xnat_after[t])])
        elif not is_dt:
            ss = P.val(SE_S)
            z_store_after[oi] = dma("scalar", z_scr[ot, :, :], zst[:, b, 0:T],
                                    waits=[(SE_S, ss)])
    s_p3 = (P.val(SE_V), P.val(SE_S))
    vz_done = P.val(SQ_S)

    P.emit("gpsimd", lambda e: e.memset(drow[0:NH, 1, 0:1], 0.0),
           waits=[(SE_S, s_p3[1])], inc=SE_G)
    P.emit("gpsimd", lambda e: e.memset(stg[0:NH, 0, 0:T], 1.0),
           waits=[(SE_V, s_p3[0]), (SE_S, s_p3[1])], inc=SE_G)
    sg = P.val(SE_G)
    vop(lambda e: e.tensor_tensor_scan(
        drow[0:NH, 1, 1:1 + T], stg[0:NH, 0, 0:T], drow[0:NH, 0, 0:T],
        0.0, ALU.mult, ALU.add), waits=[(SE_G, sg), (SE_S, s_p3[1])], dr=True)
    vop(lambda e: e.tensor_scalar_mul(drow[0:NH, 1, 1:1 + T],
                                      drow[0:NH, 1, 1:1 + T], arow[0:NH, 0:1]),
        dr=True)
    sv = vop(lambda e: e.reduce_sum(arow[0:NH, 4:5], drow[0:NH, 0, 0:T],
                                    axis=AX.X), dr=True)
    s_dt = (P.val(SE_V), P.val(SE_S))

    def chunk_smalls(c):
        lc = drow[0:NH, 1, 1 + c * Q:1 + (c + 1) * Q]
        lprev = drow[0:NH, 1, c * Q:c * Q + 1]
        lend = drow[0:NH, 1, (c + 1) * Q:(c + 1) * Q + 1]
        ss = sop(lambda e: e.activation(scr64[0:NH, 0:Q], lc, ACT.Exp,
                                        scale=-1.0, bias=lend),
                 waits=[(SE_V, s_dt[0]), (SE_S, s_dt[1]),
                        (SE_T, P.val(SE_T))])
        vop(lambda e, c=c: e.tensor_mul(scr64[0:NH, 0:Q], scr64[0:NH, 0:Q],
                                        drow[0:NH, 0, c * Q:(c + 1) * Q]),
            waits=[(SE_S, ss)], dr=True)
        sv = vop(lambda e: e.tensor_scalar_mul(arow[0:NH, 5:6], lprev, -1.0),
                 dr=True)
        sop(lambda e: e.activation(scr64[0:NH, Q:2 * Q], lc, ACT.Exp,
                                   bias=arow[0:NH, 5:6]), waits=[(SE_V, sv)],
            dr=True)
        sop(lambda e: e.activation(arow[0:NH, 5:6], lend, ACT.Exp,
                                   bias=arow[0:NH, 5:6]), dr=True)
        ss = sop(lambda e, c=c: e.activation(
            scr64[0:NH, 2 * Q:3 * Q], drow[0:NH, 0, c * Q:(c + 1) * Q], ACT.Ln))
        pairs = [(scr64[0:NH, 0:Q], alphc), (scr64[0:NH, Q:2 * Q], expuc),
                 (lc, ltc), (scr64[0:NH, 2 * Q:3 * Q], lndtc)]
        for pi, (src, dst) in enumerate(pairs):
            stv = top(lambda e, src=src, pi=pi: e.matmul(
                PDH[:, pi * NH:(pi + 1) * NH], src, idf[:, :],
                is_transpose=True),
                waits=[(SE_S, ss), (SE_V, P.val(SE_V))] if pi == 0 else
                [(SE_V, P.val(SE_V))])
            vop(lambda e, dst=dst, pi=pi: e.tensor_copy(
                dst[:, 0:NH], PDH[:, pi * NH:(pi + 1) * NH]),
                waits=[(SE_T, stv)])
        v1 = dma("scalar", rows_scr[2:3, 0:NH], arow[0:NH, 5:6],
                 waits=[(SE_S, ss)])
        gv = dma("gpsimd", gamt[:, 0:NH],
                 bass.AP(rows_scr, 2 * RW, [[0, 128], [1, NH]]),
                 waits=[(SQ_S, v1)])
        return P.val(SE_V), gv

    def hbc(tile):
        return tile[:, 0:NH].unsqueeze(2).broadcast_to([128, NH, HD])

    xns_load = {}

    def load_xns(c, extra=()):
        xns_load[c] = dma("sync", xns[:, c % 2, :],
                          xnat_scr[c].rearrange("c p q -> p (c q)") if False
                          else xnat_scr[c, :, :],
                          waits=list(extra))

    def build_xtil(c, extra=()):
        vop(lambda e, c=c: e.tensor_tensor(
            xtil[:, 0:DS].rearrange("p (a b) -> p a b", b=HD),
            xns[:, c % 2, :].rearrange("p (a b) -> p a b", b=HD),
            hbc(alphc), ALU.mult),
            waits=list(extra) + [(SE_V, P.val(SE_V)), (SQ_W, xns_load[c])],
            dr=True)

    def bnat_of(c, extra=()):
        stv = top(lambda e, c=c: e.matmul(PTB[:, 0:128], bt[:, c * Q:(c + 1) * Q],
                                          idbf[:, :], is_transpose=True),
                  waits=list(extra) + [(SE_S, s_dt[1]), (SE_V, P.val(SE_V))])
        vop(lambda e: e.tensor_copy(bnat[:, :], PTB[:, 0:128]),
            waits=[(SE_T, stv)])

    def dh_update(c, hcur, hnew, gv):
        ng = DS // 512
        for g in range(ng):
            stv = top(lambda e, g=g: e.matmul(
                PDH[:, 0:512], bnat[:, :], xtil[:, g * 512:(g + 1) * 512],
                start=True, stop=True), waits=[(SE_V, P.val(SE_V))])
            hpg = 512 // HD
            vop(lambda e, g=g, hcur=hcur: e.tensor_tensor(
                stg[:, 0, 0:512].rearrange("p (a b) -> p a b", b=HD),
                hcur[:, g * 512:(g + 1) * 512].rearrange("p (a b) -> p a b", b=HD),
                gamt[:, g * hpg:(g + 1) * hpg].unsqueeze(2).broadcast_to(
                    [128, hpg, HD]), ALU.mult),
                waits=[(SE_T, stv), (SQ_G, gv)], dr=True)
            vop(lambda e, g=g, hnew=hnew: e.tensor_add(
                hnew[:, g * 512:(g + 1) * 512], stg[:, 0, 0:512], PDH[:, 0:512]),
                dr=True)

    if getattr(cfg, "stop_after", 99) < 4:
        return _finish(nc, P)
    P.emit("gpsimd", lambda e: e.memset(H0[:, 0:DS], 0.0),
           waits=[(SE_S, s_use[0]), (SE_S, s_use[1])], inc=SE_G)
    sg = P.val(SE_G)
    P.wait("vector", [(SE_G, sg)])
    hb = [H0, H1]
    for c in range(NCH):
        load_xns(c, [(SQ_S, xblk_store[XT - 3])] if c == 0 else
                 [(SE_V, P.val(SE_V))])
        sv, gv = chunk_smalls(c)
        bnat_of(c)
        build_xtil(c, [(SE_V, s_p3[0])])
        dh_update(c, hb[c % 2], hb[(c + 1) % 2], gv)
    hfin = hb[NCH % 2]
    sv = P.val(SE_V)
    P.emit("gpsimd", lambda e: e.memset(stg[0:1, 1, 0:1024], 0.0),
           waits=[(SE_S, P.val(SE_S))], inc=SE_G)
    sgz2 = P.val(SE_G)
    for seg in range(0, DS, 1024):
        dma("scalar", ag2_in[S:S + 1, seg:seg + min(1024, DS - seg)],
            stg[0:1, 1, 0:min(1024, DS - seg)], waits=[(SE_G, sgz2)])
    v1 = dma("scalar", ag2_in[0:S, :], hfin[0:S, 0:DS], waits=[(SE_V, sv)])
    v2 = dma("scalar", ag2_in[S:S + 1, 0:NH], arow[0:NH, 4:5])
    P.emit("gpsimd",
           lambda e: e.collective_compute(
               "AllGather", ALU.bypass, replica_groups=[list(range(NC))],
               ins=[ag2_in[:, :].opt()], outs=[ag2_out[:, :].opt()]),
           waits=[(SQ_S, v2)], inc=SE_C)
    agc2 = P.val(SE_C)

    if getattr(cfg, "stop_after", 99) < 5:
        return _finish(nc, P)
    v1 = dma("gpsimd", arow2[0:NH, 0:NC],
             bass.AP(ag2_out, S * DS, [[1, NH], [(S + 1) * DS, NC]]),
             waits=[(SE_C, agc2)])
    P.emit("gpsimd", lambda e: e.memset(stg[0:NH, 0, 0:NC], 1.0),
           waits=[(SE_V, P.val(SE_V))], inc=SE_G)
    sg5 = P.val(SE_G)
    vop(lambda e: e.tensor_tensor_scan(
        wcomb[0:NH, 0:NC], stg[0:NH, 0, 0:NC], arow2[0:NH, 0:NC],
        0.0, ALU.mult, ALU.add), waits=[(SQ_G, v1), (SE_G, sg5)], dr=True)
    vop(lambda e: e.tensor_mul(arow2[0:NH, 0:NC], arow2[0:NH, 0:NC],
                               segm_t[0:NH, 0:NC]),
        waits=[(SQ_G, g_const)], dr=True)
    vop(lambda e: e.reduce_sum(arow[0:NH, 5:6], arow2[0:NH, 0:NC], axis=AX.X),
        dr=True)
    sv = vop(lambda e: e.tensor_scalar(wcomb[0:NH, 0:NC], wcomb[0:NH, 0:NC],
                                       arow[0:NH, 5:6], 0.0,
                                       ALU.subtract, ALU.min), dr=True)
    ss = sop(lambda e: e.activation(wcomb[0:NH, 0:NC], wcomb[0:NH, 0:NC], ACT.Exp,
                                    scale=arow[0:NH, 1:2]), waits=[(SE_V, sv)])
    sv = vop(lambda e: e.tensor_mul(wcomb[0:NH, 0:NC], wcomb[0:NH, 0:NC],
                                    segm_t[0:NH, 0:NC]), waits=[(SE_S, ss)], dr=True)
    stv = top(lambda e: e.matmul(PDH[0:NC, 0:NH], wcomb[0:NH, 0:NC], idf[:, :],
                                 is_transpose=True), waits=[(SE_V, sv)])
    sv = vop(lambda e: e.tensor_copy(wct[0:NC, 0:NH], PDH[0:NC, 0:NH]),
             waits=[(SE_T, stv)])
    v1 = dma("scalar", rows_scr[8:8 + NC, 0:NH], wct[0:NC, 0:NH],
             waits=[(SE_V, sv)])
    P.emit("gpsimd", lambda e: e.memset(H0[:, 0:DS], 0.0),
           waits=[(SQ_S, v1)], inc=SE_G)
    sg = P.val(SE_G)
    P.wait("vector", [(SE_G, sg)])
    for j in range(NC):
        vw = dma("gpsimd", hin_t[:, 0:NH],
                 bass.AP(rows_scr, (8 + j) * RW, [[0, 128], [1, NH]]),
                 waits=[(SQ_S, v1), (SE_V, P.val(SE_V))])
        vs = dma("sync", H1[0:S, 0:DS], ag2_out[j * (S + 1):j * (S + 1) + S, :],
                 waits=[(SE_V, P.val(SE_V))])
        vop(lambda e: e.tensor_tensor(
            H1[:, 0:DS].rearrange("p (a b) -> p a b", b=HD),
            H1[:, 0:DS].rearrange("p (a b) -> p a b", b=HD),
            hbc(hin_t), ALU.mult), waits=[(SQ_G, vw), (SQ_W, vs)], dr=True)
        vop(lambda e: e.tensor_add(H0[:, 0:DS], H0[:, 0:DS], H1[:, 0:DS]), dr=True)
    s_hin = P.val(SE_V)
    if getattr(cfg, "dbg", False):
        dma("scalar", dbg_hin[:, :], H0[:, 0:DS], waits=[(SE_V, s_hin)])

    if getattr(cfg, "stop_after", 99) < 6:
        return _finish(nc, P)
    QW = min(DS, 1024)
    NQ = DS // QW
    HGRP = min(LBW // Q, NH)
    hb2 = [H0, H1]
    for c in range(NCH):
        load_xns(c, [(SE_V, P.val(SE_V))])
        sv, gv = chunk_smalls(c)
        bnat_of(c)
        stv = top(lambda e, c=c: e.matmul(
            PDH[:, 0:128], bt[:, c * Q:(c + 1) * Q], ct[:, c * Q:(c + 1) * Q],
            start=True, stop=True),
            waits=[(SE_V, P.val(SE_V))])
        vop(lambda e: e.tensor_copy(gbf[:, :], PDH[:, 0:128]),
            waits=[(SE_T, stv)])
        build_xtil(c)
        hcur, hnew = hb2[c % 2], hb2[(c + 1) % 2]
        vop(lambda e, hcur=hcur: e.tensor_copy(hbf[:, 0:DS], hcur[:, 0:DS]),
            waits=[(SE_V, s_hin if c == 0 else P.val(SE_V))], dr=True)
        lb_v = 0
        for q in range(NQ):
            f0 = q * QW
            h0h = f0 // HD
            nheads = QW // HD
            for g in range(QW // 512):
                top(lambda e, c=c, f0=f0, g=g: e.matmul(
                    PYB[:, g * 512:(g + 1) * 512], ct[:, c * Q:(c + 1) * Q],
                    hbf[:, f0 + g * 512:f0 + (g + 1) * 512],
                    start=True, stop=True),
                    waits=[(SE_V, P.val(SE_V))] if g == 0 else [])
            for hh in range(nheads):
                h = h0h + hh
                hl = h % HGRP
                if h % HGRP == 0:
                    hg = h // HGRP
                    vv1 = dma("scalar", rows_scr[3:4, 0:HGRP * Q],
                              bass.AP(drow, (hg * HGRP) * 2 * (T + 2) + (T + 2)
                                      + 1 + c * Q,
                                      [[2 * (T + 2), HGRP], [1, Q]]),
                              waits=[(SE_V, P.val(SE_V))])
                    lb_v = dma("gpsimd", LBC[:, 0:HGRP * Q],
                               bass.AP(rows_scr, 3 * RW,
                                       [[0, 128], [1, HGRP * Q]]),
                               waits=[(SQ_S, vv1), (SE_V, P.val(SE_V))])
                es, ws = hh % 2, hh % 4
                wts = [(SQ_G, lb_v)]
                if hh >= 2:
                    wts.append((SE_S, P.val(SE_S)))
                if hh >= 4:
                    wts.append((SE_T, P.val(SE_T)))
                vop(lambda e, hl=hl, h=h, es=es: e.scalar_tensor_tensor(
                    etmp[:, es, :], LBC[:, hl * Q:(hl + 1) * Q],
                    ltc[:, h:h + 1], tri[:, :], ALU.subtract, ALU.add),
                    waits=wts)
                sv = P.val(SE_V)
                ss = sop(lambda e, h=h, es=es: e.activation(
                    ebf[:, es, :], etmp[:, es, :], ACT.Exp,
                    bias=lndtc[:, h:h + 1]), waits=[(SE_V, sv)])
                vop(lambda e, es=es, ws=ws: e.tensor_mul(
                    wbf[:, ws, :], ebf[:, es, :], gbf[:, :]),
                    waits=[(SE_S, ss)], dr=True)
                sv = vop(lambda e, h=h, ws=ws: e.scalar_tensor_tensor(
                    wbf[:, ws, :], idbf[:, :], dhead[:, h:h + 1], wbf[:, ws, :],
                    ALU.mult, ALU.add), dr=True)
                top(lambda e, c=c, h=h, hh=hh, ws=ws: e.matmul(
                    PYA[:, hh * HD:(hh + 1) * HD], wbf[:, ws, :],
                    xns[:, c % 2, h * HD:(h + 1) * HD],
                    start=True, stop=True),
                    waits=[(SE_V, sv), (SQ_W, xns_load[c])])
            stv = P.val(SE_T)
            vop(lambda e, q=q, h0h=h0h, nheads=nheads: e.tensor_tensor(
                stg[:, 0, 0:QW].rearrange("p (a b) -> p a b", b=HD),
                PYB[:, 0:QW].rearrange("p (a b) -> p a b", b=HD),
                expuc[:, h0h:h0h + nheads].unsqueeze(2).broadcast_to(
                    [128, nheads, HD]), ALU.mult),
                waits=[(SE_T, stv)], dr=True)
            vop(lambda e, c=c, f0=f0: e.tensor_add(
                ynat(c)[:, f0:f0 + QW], stg[:, 0, 0:QW], PYA[:, 0:QW]), dr=True)
        dh_update(c, hcur, hnew, gv)
    s_y = P.val(SE_V)
    if getattr(cfg, "dbg", False):
        dma("scalar", dbg_y[:, :], poolC[:, 0:NCH * DS], waits=[(SE_V, s_y)])

    if getattr(cfg, "stop_after", 99) < 7:
        return _finish(nc, P)
    NOT_ = D // 128
    for o in range(NOT_):
        b = o % 2
        wts = [(SE_V, P.val(SE_V))] if o >= 2 else [(SE_V, s_y)]
        vv = dma("sync", fbig[b][:, 0:D], ffw[o * 128:(o + 1) * 128, :], waits=wts)
        vop(lambda e, b=b, o=o: e.reduce_max(
            amx[:, 0:1] if o == 0 else amx[:, 1:2], fbig[b][:, 0:D],
            axis=AX.X, apply_absolute_value=True),
            waits=[(SQ_W, vv)], dr=True)
        if o > 0:
            vop(lambda e: e.tensor_max(amx[:, 0:1], amx[:, 0:1], amx[:, 1:2]),
                dr=True)
    sv = P.val(SE_V)
    v1 = dma("scalar", rows_scr[6:7, 0:128], amx[:, 0:1], waits=[(SE_V, sv)])
    v2 = dma("gpsimd", rowt[0:1, 1040:1168],
             bass.AP(rows_scr, 6 * RW, [[0, 1], [1, 128]]), waits=[(SQ_S, v1)])
    vop(lambda e: e.reduce_max(rowt[0:1, 1168:1169], rowt[0:1, 1040:1168], axis=AX.X),
        waits=[(SQ_G, v2)], dr=True)
    vop(lambda e: e.reciprocal(rowt[0:1, 1169:1170], rowt[0:1, 1168:1169]), dr=True)
    sv = vop(lambda e: e.tensor_scalar_mul(rowt[0:1, 1170:1171], rowt[0:1, 1169:1170],
                                           224.0), dr=True)
    v1 = dma("scalar", rows_scr[7:8, 0:3], rowt[0:1, 1168:1171], waits=[(SE_V, sv)])
    vqs = dma("gpsimd", amx[:, 2:3],
              bass.AP(rows_scr, 7 * RW + 2, [[0, 128], [1, 1]]),
              waits=[(SQ_S, v1)])
    wq8_store = {}
    for o in range(NOT_):
        b = o % 2
        wts = [(SE_V, P.val(SE_V))]
        vv = dma("sync", fbig[b][:, 0:D], ffw[o * 128:(o + 1) * 128, :], waits=wts)
        sv = vop(lambda e, b=b: e.tensor_scalar_mul(
            wq8[:, 0, :], fbig[b][:, 0:D], amx[:, 2:3]),
            waits=[(SQ_W, vv), (SQ_G, vqs), (SQ_G, wq8_store.get(o - 1, 0))])
        wq8_store[o] = dma("gpsimd", wqbf[o * 128:(o + 1) * 128, :], wq8[:, 0, :],
                           waits=[(SE_V, sv)])
    vwq_done = P.val(SQ_G)

    s_tr_evac = 0
    tr_evac = {}
    for ft in range(KT):
        b = ft % 2
        vz = dma("scalar", zst[:, b, 0:T], z_scr[ft, :, :],
                 waits=[(SE_V, tr_evac.get(ft - 2, 0)), (SQ_S, vz_done)])
        ss = sop(lambda e, b=b: e.activation(zst[:, b, 0:T], zst[:, b, 0:T],
                                             ACT.Silu),
                 waits=[(SQ_S, vz), (SE_V, s_y)])
        for c in range(NCH):
            slot = c % 8
            wts = [(SE_S, ss)] if c == 0 else [(SE_V, P.val(SE_V))]
            if c == 0 and ft > 0:
                wts.append((SE_V, s_tr_evac))
            stv = top(lambda e, c=c, ft=ft, slot=slot: e.matmul(
                PTB[:, slot * 128:(slot + 1) * 128],
                ynat(c)[:, ft * 128:(ft + 1) * 128], idbf[:, :],
                is_transpose=True), waits=wts)
            vop(lambda e, c=c, ft=ft, slot=slot, b=b: e.tensor_mul(
                rhs3_slab(ft, c)[:, 0:128], PTB[:, slot * 128:(slot + 1) * 128],
                zst[:, b, c * 128:(c + 1) * 128]),
                waits=[(SE_T, stv)])
        s_tr_evac = P.val(SE_V)
        tr_evac[ft] = s_tr_evac
    s_yg = P.val(SE_V)
    for st in range(NCH):
        for ft in range(KT):
            sq = ft % 2
            wts = [(SE_V, s_yg)] if ft == 0 else []
            if ft >= 2:
                wts = wts + [(SE_T, P.val(SE_T))]
            ss = sop(lambda e, st=st, ft=ft, sq=sq: e.activation(
                sqt[:, sq, :], rhs3_slab(ft, st)[:, 0:128], ACT.Square),
                waits=wts)
            top(lambda e, ft=ft, sq=sq: e.matmul(
                PA[0:1, 0:128], onec[:, 0:1], sqt[:, sq, :],
                start=(ft == 0), stop=(ft == KT - 1)), waits=[(SE_S, ss)])
        stv = P.val(SE_T)
        sop(lambda e, st=st: e.activation(
            rowt[0:1, st * 128:(st + 1) * 128], PA[0:1, 0:128], ACT.Ln,
            bias=epsc[0:1, 0:1], scale=1.0 / DS),
            waits=[(SE_T, stv), (SE_G, sg_eps)], dr=True)
        sop(lambda e, st=st: e.activation(
            rowt[0:1, st * 128:(st + 1) * 128],
            rowt[0:1, st * 128:(st + 1) * 128], ACT.Exp, scale=-0.5))
    ss = P.val(SE_S)
    v1 = dma("scalar", rows_scr[1:2, 0:T], rowt[0:1, 0:T], waits=[(SE_S, ss)])
    v2 = dma("gpsimd", LBC[:, 0:T], bass.AP(rows_scr, RW, [[0, 128], [1, T]]),
             waits=[(SQ_S, v1)])
    for ft in range(KT):
        for st in range(NCH):
            vop(lambda e, ft=ft, st=st: e.scalar_tensor_tensor(
                rhs3_slab(ft, st)[:, 0:128], LBC[:, st * 128:(st + 1) * 128],
                nw_c[:, ft:ft + 1], rhs3_slab(ft, st)[:, 0:128],
                ALU.mult, ALU.mult),
                waits=[(SQ_G, v2)] if (ft == 0 and st == 0) else [])
    s_rhs3 = P.val(SE_V)
    if getattr(cfg, "dbg", False):
        dma("scalar", dbg_r3[:, :], poolC[:, 0:NCH * DS], waits=[(SE_V, s_rhs3)])

    if getattr(cfg, "stop_after", 99) < 8:
        return _finish(nc, P)
    P.emit("gpsimd", lambda e: e.memset(sumc[:, :, :], 0.0),
           waits=[(SE_V, s_rhs3)], inc=SE_G)
    sgz = P.val(SE_G)
    OM = D // 256
    t_om = []
    v_om = []
    for om in range(OM):
        b = om % 2
        wts = [(SE_T, t_om[om - 2])] if om >= 2 else []
        vv = dma("gpsimd", womb[b],
                 bass.AP(opw, om * 256, [[D, 128], [128 * D, KT], [1, 256]]),
                 waits=wts)
        vg_ = dma("gpsimd", gaB[:, :],
                  bass.AP(rows_scr, 4 * RW + om * 256, [[0, 128], [1, 256]]),
                  waits=[(SE_V, v_om[om - 1])] if om >= 1 else
                  [(SQ_G, g_rows45)])
        for st in range(NCH):
            ps = ps_pair[(om * NCH + st) % 2]
            for ft in range(KT):
                wts = [(SQ_G, vv), (SE_V, s_rhs3)] if ft == 0 else []
                if ft == 0:
                    wts.append((SE_V, P.val(SE_V)))
                top(lambda e, b=b, st=st, ft=ft, ps=ps: e.matmul(
                    ps[:, 0:256], rhs3_slab(ft, st)[:, 0:128],
                    womb[b][:, ft, :],
                    start=(ft == 0), stop=(ft == KT - 1)), waits=wts)
            stv = P.val(SE_T)
            vx = dma("sync", stg[:, 1, (st % 2) * 512:(st % 2) * 512 + 256],
                     x_in[HALO + st * 128:HALO + (st + 1) * 128,
                          om * 256:(om + 1) * 256],
                     waits=[(SE_V, P.val(SE_V))])
            vop(lambda e, ps=ps: e.tensor_tensor(
                stg[:, 0, 0:256], ps[:, 0:256], gaB[:, :], ALU.mult),
                waits=[(SE_T, stv), (SQ_G, vg_), (SQ_W, vx), (SE_G, sgz),
                       (SQ_S, P.val(SQ_S))],
                dr=True)
            vop(lambda e, st=st: e.scalar_tensor_tensor(
                stg[:, 0, 256:512], stg[:, 0, 0:256], 1.0,
                stg[:, 1, (st % 2) * 512:(st % 2) * 512 + 256],
                ALU.mult, ALU.add, accum_out=statc[:, 9:10]), dr=True)
            vop(lambda e: e.scalar_tensor_tensor(
                stg[:, 0, 512:768], stg[:, 0, 256:512], 1.0,
                stg[:, 0, 256:512], ALU.mult, ALU.mult,
                accum_out=statc[:, 10:11]), dr=True)
            vop(lambda e, st=st: e.tensor_add(
                sumc[:, 0, st:st + 1], sumc[:, 0, st:st + 1], statc[:, 9:10]),
                dr=True)
            sv = vop(lambda e, st=st: e.tensor_add(
                sumc[:, 1, st:st + 1], sumc[:, 1, st:st + 1], statc[:, 10:11]),
                dr=True)
            dma("scalar", x2_scr[st * 128:(st + 1) * 128, om * 256:(om + 1) * 256],
                stg[:, 0, 256:512], waits=[(SE_V, sv)])
        t_om.append(P.val(SE_T))
        v_om.append(P.val(SE_V))
    s_x2 = P.val(SE_V)
    vx2 = P.val(SQ_S)
    for st in range(NCH):
        vop(lambda e, st=st: e.tensor_scalar_mul(
            statc[:, 12:13], sumc[:, 0, st:st + 1], 1.0 / D),
            waits=[(SE_V, s_x2)] if st == 0 else [], dr=True)
        vop(lambda e: e.scalar_tensor_tensor(
            statc[:, 13:14], statc[:, 12:13], -1.0, statc[:, 12:13],
            ALU.mult, ALU.mult), dr=True)
        sv = vop(lambda e, st=st: e.scalar_tensor_tensor(
            statc[:, 14:15], sumc[:, 1, st:st + 1], 1.0 / D, statc[:, 13:14],
            ALU.mult, ALU.add), dr=True)
        sop(lambda e: e.activation(statc[:, 15:16], statc[:, 14:15], ACT.Ln,
                                   bias=epsc[:, 0:1]),
            waits=[(SE_V, sv), (SE_G, sg_eps)], dr=True)
        ss = sop(lambda e, st=st: e.activation(
            statc[:, 16 + st:17 + st], statc[:, 15:16], ACT.Exp, scale=-0.5))
        vop(lambda e, st=st: e.scalar_tensor_tensor(
            statc[:, 24 + st:25 + st], statc[:, 12:13], -1.0,
            statc[:, 16 + st:17 + st], ALU.mult, ALU.mult),
            waits=[(SE_S, ss)], dr=True)
    s_ln2 = (P.val(SE_V), P.val(SE_S))
    P.emit("gpsimd", lambda e: e.memset(amx[:, 4:5], 0.0),
           waits=[(SE_V, P.val(SE_V))], inc=SE_G)
    sgz = P.val(SE_G)
    NDC = (D + CBW - 1) // CBW
    for st in range(NCH):
        vx = dma("sync", h2tile[:, 0:D], x2_scr[st * 128:(st + 1) * 128, :],
                 waits=[(SQ_S, vx2)] if st == 0 else
                 [(SQ_S, P.val(SQ_S)), (SE_V, P.val(SE_V))])
        ss = sop(lambda e, st=st: e.activation(
            h2tile[:, 0:D], h2tile[:, 0:D], ACT.Identity,
            scale=statc[:, 16 + st:17 + st], bias=statc[:, 24 + st:25 + st]),
            waits=[(SQ_W, vx), (SE_V, s_ln2[0]), (SE_S, s_ln2[1]),
                   (SE_G, sgz)])
        for dc in range(NDC):
            d0 = dc * CBW
            dl = min(CBW, D - d0)
            r0d, c0d = divmod(d0, SLC)
            gcb = dma("gpsimd", cb1B[:, 0:dl],
                      bass.AP(ag1_out, (r0d * 6 + 4) * SLC + c0d,
                              [[0, 128], [6 * SLC, dl // SLC], [1, SLC]])
                      if dl >= SLC else
                      bass.AP(ag1_out, (r0d * 6 + 4) * SLC + c0d,
                              [[0, 128], [1, dl]]),
                      waits=[(SE_V, P.val(SE_V))])
            gsb = dma("gpsimd", sbB[:, 0:dl],
                      bass.AP(ag1_out, (r0d * 6 + 3) * SLC + c0d,
                              [[0, 128], [6 * SLC, dl // SLC], [1, SLC]])
                      if dl >= SLC else
                      bass.AP(ag1_out, (r0d * 6 + 3) * SLC + c0d,
                              [[0, 128], [1, dl]]))
            vop(lambda e, dl=dl: e.tensor_scalar_add(cb1B[:, 0:dl],
                                                     cb1B[:, 0:dl], 1.0),
                waits=[(SQ_G, gsb), (SE_S, ss)], dr=True)
            vop(lambda e, d0=d0, dl=dl: e.tensor_mul(
                h2tile[:, d0:d0 + dl], h2tile[:, d0:d0 + dl], cb1B[:, 0:dl]),
                dr=True)
            vop(lambda e, d0=d0, dl=dl: e.tensor_add(
                h2tile[:, d0:d0 + dl], h2tile[:, d0:d0 + dl], sbB[:, 0:dl]),
                dr=True)
        vop(lambda e: e.reduce_max(statc[:, 11:12], h2tile[:, 0:D], axis=AX.X,
                                   apply_absolute_value=True), dr=True)
        sv = vop(lambda e: e.tensor_max(amx[:, 4:5], amx[:, 4:5],
                                        statc[:, 11:12]), dr=True)
        dma("scalar", h2_scr[st * 128:(st + 1) * 128, :], h2tile[:, 0:D],
            waits=[(SE_V, sv)])
    vh2 = P.val(SQ_S)
    sv = P.val(SE_V)
    v1 = dma("scalar", rows_scr[6:7, 0:128], amx[:, 4:5], waits=[(SE_V, sv)])
    v2 = dma("gpsimd", rowt[0:1, 1040:1168],
             bass.AP(rows_scr, 6 * RW, [[0, 1], [1, 128]]), waits=[(SQ_S, v1)])
    sv = vop(lambda e: e.reduce_max(rowt[0:1, 1172:1173], rowt[0:1, 1040:1168],
                                    axis=AX.X), waits=[(SQ_G, v2)], dr=True)
    P.emit("gpsimd", lambda e: e.memset(rowt[0:1, 1330:1346], 0.0), inc=SE_G)
    sgz3 = P.val(SE_G)
    dma("scalar", ag3_in[0:1, 0:16], rowt[0:1, 1330:1346], waits=[(SE_G, sgz3)])
    v1 = dma("scalar", ag3_in[0:1, 0:1], rowt[0:1, 1172:1173], waits=[(SE_V, sv)])
    P.emit("gpsimd",
           lambda e: e.collective_compute(
               "AllGather", ALU.bypass, replica_groups=[list(range(NC))],
               ins=[ag3_in[:, :].opt()], outs=[ag3_out[:, :].opt()]),
           waits=[(SQ_S, v1)], inc=SE_C)
    agc3 = P.val(SE_C)
    v2 = dma("gpsimd", rowt[0:1, 1196:1196 + 16 * NC],
             bass.AP(ag3_out, 0, [[0, 1], [1, 16 * NC]]), waits=[(SE_C, agc3)])
    vop(lambda e: e.reduce_max(
        rowt[0:1, 1173:1174],
        bass.AP(rowt, 1196, [[RWT, 1], [16, NC]]), axis=AX.X),
        waits=[(SQ_G, v2)], dr=True)
    vop(lambda e: e.reciprocal(rowt[0:1, 1174:1175], rowt[0:1, 1173:1174]), dr=True)
    vop(lambda e: e.tensor_scalar_mul(rowt[0:1, 1175:1176], rowt[0:1, 1174:1175],
                                      224.0), dr=True)
    v3 = dma("gpsimd", rowt[0:1, 1350:1351], bass.AP(rows_scr, 7 * RW, [[0, 1], [1, 1]]))
    vop(lambda e: e.tensor_mul(rowt[0:1, 1176:1177], rowt[0:1, 1173:1174],
                               rowt[0:1, 1350:1351]), waits=[(SQ_G, v3)], dr=True)
    sv = vop(lambda e: e.tensor_scalar_mul(
        rowt[0:1, 1177:1178], rowt[0:1, 1176:1177], 4.0 / (448.0 * 448.0)), dr=True)
    v1 = dma("scalar", rows_scr[7:8, 4:8], rowt[0:1, 1174:1178], waits=[(SE_V, sv)])
    dma("gpsimd", amx[:, 5:6], bass.AP(rows_scr, 7 * RW + 5, [[0, 128], [1, 1]]),
        waits=[(SQ_S, v1)])
    g_scales = dma("gpsimd", amx[:, 6:7],
                   bass.AP(rows_scr, 7 * RW + 7, [[0, 128], [1, 1]]))

    if getattr(cfg, "stop_after", 99) < 10:
        return _finish(nc, P)
    for st in range(NCH):
        b = st % 2
        vx = dma("sync", fbig[b][:, 0:D], h2_scr[st * 128:(st + 1) * 128, :],
                 waits=[(SQ_S, vh2)] if st == 0 else [(SE_V, P.val(SE_V))])
        sv = vop(lambda e, b=b: e.tensor_scalar_mul(
            h2q8[:, 0, :], fbig[b][:, 0:D], amx[:, 5:6]),
            waits=[(SQ_W, vx), (SQ_G, g_scales), (SQ_G, P.val(SQ_G))])
        dma("gpsimd", h2bf[st * 128:(st + 1) * 128, :], h2q8[:, 0, :],
            waits=[(SE_V, sv)])
    vhbf = P.val(SQ_G)
    for kt in range(KT):
        dma("scalar", h2qT(kt)[:, 0:T], h2bf[0:T, kt * 128:(kt + 1) * 128],
            waits=[(SQ_G, vhbf), (SE_V, P.val(SE_V))] if kt == 0 else [],
            transpose=True)
    vh2t = P.val(SQ_S)
    t_om, v_om = [], []
    for om in range(OM):
        b = om % 2
        qsel = "scalar" if om % 2 == 0 else "sync"
        qk = QKEY[qsel]
        for kt in range(KT):
            wts = []
            if kt == 0:
                wts = [(SQ_S, vh2t), (SQ_G, vwq_done)]
                if om >= 2:
                    wts.append((SE_T, t_om[om - 2]))
            dma(qsel, womb[b][:, kt, :],
                wqbf[om * 256:(om + 1) * 256, kt * 128:(kt + 1) * 128],
                waits=wts, transpose=True)
        vwq = (qk, P.val(qk))
        vg1 = dma("gpsimd", gbB[:, :],
                  bass.AP(rows_scr, 5 * RW + om * 256, [[0, 128], [1, 256]]),
                  waits=[(SE_V, v_om[om - 1])] if om >= 1 else [])
        vg2 = dma("gpsimd", bofs[:, :],
                  vsl("ffnb", om * 256, (om + 1) * 256).broadcast_to([128, 256]))
        for st in range(NCH):
            ps = ps_pair[(om * NCH + st) % 2]
            for kt in range(KT):
                wts = [vwq] if kt == 0 else []
                if kt == 0:
                    wts.append((SE_V, P.val(SE_V)))
                top(lambda e, b=b, st=st, kt=kt, ps=ps: e.matmul(
                    ps[:, 0:256], h2qT(kt)[:, st * 128:(st + 1) * 128],
                    womb[b][:, kt, :], start=(kt == 0), stop=(kt == KT - 1)),
                    waits=wts)
            stv = P.val(SE_T)
            vx = dma("sync", stg[:, 1, (st % 2) * 512:(st % 2) * 512 + 256],
                     x2_scr[st * 128:(st + 1) * 128, om * 256:(om + 1) * 256],
                     waits=[(SE_V, P.val(SE_V))])
            vop(lambda e, ps=ps: e.tensor_scalar_mul(
                stg[:, 0, 0:256], ps[:, 0:256], amx[:, 6:7]),
                waits=[(SE_T, stv), (SQ_G, vg2), (SQ_W, vx),
                       (SQ_S, P.val(SQ_S))], dr=True)
            vop(lambda e: e.tensor_add(stg[:, 0, 0:256], stg[:, 0, 0:256],
                                       bofs[:, :]), dr=True)
            vop(lambda e: e.tensor_mul(stg[:, 0, 0:256], stg[:, 0, 0:256],
                                       gbB[:, :]), dr=True)
            sv = vop(lambda e, st=st: e.tensor_add(
                stg[:, 0, 256:512], stg[:, 0, 0:256],
                stg[:, 1, (st % 2) * 512:(st % 2) * 512 + 256]), dr=True)
            dma("scalar", out[st * 128:(st + 1) * 128, om * 256:(om + 1) * 256],
                stg[:, 0, 256:512], waits=[(SE_V, sv)])
        t_om.append(P.val(SE_T))
        v_om.append(P.val(SE_V))
    v_out = P.val(SQ_S)
    P.wait("gpsimd", [(SQ_S, v_out)])
    P.wait("sync", [(SQ_S, v_out)])
    P.wait("tensor", [(SQ_S, v_out)])

    return _finish(nc, P)


def _finish(nc, P):
    import contextlib
    with contextlib.ExitStack() as es:
        for s in ALL_SEMS:
            P.sems[s] = es.enter_context(nc.semaphore(s))
        with nc.Block() as block:
            @block.sync
            def _(e):
                for fn in P.ops["sync"]:
                    fn(e)

            @block.gpsimd
            def _(e):
                for fn in P.ops["gpsimd"]:
                    fn(e)

            @block.vector
            def _(e):
                for fn in P.ops["vector"]:
                    fn(e)

            @block.scalar
            def _(e):
                for fn in P.ops["scalar"]:
                    fn(e)

            @block.tensor
            def _(e):
                for fn in P.ops["tensor"]:
                    fn(e)
    return nc


def make_inmaps(cfg, x, ln_w, in_proj_w, conv_w, A_log, dt_bias, Dv, norm_w,
                out_proj_w, mup, ffn_w, ffn_b, cvec, wmod_f):
    x = np.ascontiguousarray(np.asarray(x, np.float32).reshape(-1, cfg.D))
    mup_pad = np.zeros(cfg.OT * 128, np.float32)
    mup_pad[:cfg.PROJ] = np.asarray(mup, np.float32)
    in_maps = []
    for c in range(cfg.NC):
        s0 = c * cfg.T
        xin = np.zeros((cfg.TH, cfg.D), np.float32)
        lo = max(0, s0 - cfg.HALO)
        xin[cfg.HALO - (s0 - lo):] = x[lo:s0 + cfg.T]
        halo = np.full(cfg.HALO, 0.0 if c == 0 else 1.0, np.float32)
        segm = (np.arange(cfg.NC) < c).astype(np.float32)
        vecs = np.concatenate([
            np.asarray(ln_w, np.float32), np.asarray(norm_w, np.float32),
            mup_pad, np.asarray(ffn_b, np.float32),
            np.asarray(A_log, np.float32), np.asarray(dt_bias, np.float32),
            np.asarray(Dv, np.float32),
            np.asarray(cvec, np.float32).reshape(-1), halo, segm])[None, :]
        wm = np.ascontiguousarray(np.concatenate(
            [np.asarray(wmod_f, np.float32)
             [:, v * cfg.D + c * cfg.SLC:v * cfg.D + (c + 1) * cfg.SLC]
             for v in range(6)], axis=1))
        in_maps.append({
            "x": np.ascontiguousarray(xin),
            "inproj": np.ascontiguousarray(np.asarray(in_proj_w, np.float32)),
            "outproj": np.ascontiguousarray(np.asarray(out_proj_w, np.float32)),
            "ffnw": np.ascontiguousarray(np.asarray(ffn_w, np.float32)),
            "wmodslc": wm,
            "convw": np.ascontiguousarray(np.asarray(conv_w, np.float32)),
            "vecs": np.ascontiguousarray(vecs),
        })
    return in_maps


def kernel(L0_attn_hidden_states, L0_attn_residual, L0_attn_ln_w,
           L0_attn_in_proj_w, L0_attn_conv_w, L0_attn_A_log, L0_attn_dt_bias,
           L0_attn_D, L0_attn_norm_w, L0_attn_out_proj_w, L0_attn_mup_vector,
           L0_ffn_weight, L0_ffn_bias, L0_c, L0_wmod):
    cfg = Cfg(D=4096, NH=64, T=1024, NC=8)
    nc_prog = build(cfg)
    in_maps = make_inmaps(
        cfg, L0_attn_hidden_states, L0_attn_ln_w, L0_attn_in_proj_w,
        L0_attn_conv_w, L0_attn_A_log, L0_attn_dt_bias, L0_attn_D,
        L0_attn_norm_w, L0_attn_out_proj_w, L0_attn_mup_vector,
        L0_ffn_weight, L0_ffn_bias, L0_c, L0_wmod)
    res = run_bass_kernel_spmd(nc_prog, in_maps, core_ids=list(range(cfg.NC)))
    outs = [np.asarray(r["out"]) for r in res.results]
    return np.concatenate(outs, axis=0).reshape(1, 8192, 4096).astype(np.float32)
